# version 14
# baseline (speedup 1.0000x reference)
import math
from contextlib import ExitStack

import numpy as np
import ml_dtypes
import concourse.bass as bass
import concourse.mybir as mybir
from concourse.bass_utils import run_bass_kernel_spmd

F32 = mybir.dt.float32
BF16 = mybir.dt.bfloat16
AF = mybir.ActivationFunctionType
ALU = mybir.AluOpType

H = 16
DM = 2048
T = 1024
NCH = 8
KC = 16
INTER = 5632
IC = 44
ICH = 22
BATCH = 4
SEQ = 2048
EPS = 1e-5

BF = ml_dtypes.bfloat16


def _lam64():
    start = 2.0 ** (-2.0 ** (-(math.log2(H) - 3)))
    slopes = np.array([start ** (i + 1) for i in range(H)], dtype=np.float64)
    return np.exp(-slopes)


def _consts():
    lam = _lam64()
    i_idx = np.arange(128, dtype=np.float64)
    dq = (lam[:, None] ** (i_idx[None, :] + 1.0)).reshape(1, H * 128)
    dq_all = np.broadcast_to(dq, (128, H * 128))
    dk = lam[None, :] ** (127.0 - i_idx[:, None])
    dk_all = np.repeat(dk, 128, axis=1)
    diff = i_idx[None, :] - i_idx[:, None]
    dm_t = np.concatenate(
        [np.where(diff >= 0, lam[h] ** np.maximum(diff, 0.0), 0.0)
         for h in range(H)], axis=1)
    lam_c = lam ** 128.0
    return (np.ascontiguousarray(dq_all.astype(BF)),
            np.ascontiguousarray(dk_all.astype(BF)),
            np.ascontiguousarray(dm_t.astype(BF)),
            [float(v) for v in lam_c])


def _tile_w(w):
    M, K = w.shape
    nm, nk = M // 128, K // 128
    t = w.reshape(nm, 128, nk, 128)
    t = np.ascontiguousarray(t.transpose(0, 3, 2, 1))
    return t.reshape(nm, 128, K)


def _tile_wd(w_down):
    t = w_down.reshape(16, 128, IC, 128)
    return np.ascontiguousarray(t.transpose(0, 3, 2, 1))


def _tile_x(xT_half):
    t = xT_half.reshape(16, 128, T).transpose(1, 0, 2)
    return np.ascontiguousarray(t)


class _Sem:
    def __init__(self, nc, name):
        self.h = nc.alloc_semaphore(name)
        self.n = 0

    def inc(self, inst, k=1):
        inst.then_inc(self.h, k)
        self.n += k
        return self.n


def build_program(taps=(), use_cc=True):
    taps = set(taps)
    nc = bass.Bass("TRN2", target_bir_lowering=False)

    x_own = nc.dram_tensor("x_own", [128, 16, T], F32, kind="ExternalInput").ap()
    wqkv_t = nc.dram_tensor("wqkv_t", [48, 128, 2048], F32, kind="ExternalInput").ap()
    wo_t = nc.dram_tensor("wo_t", [16, 128, 2048], F32, kind="ExternalInput").ap()
    wg_t = nc.dram_tensor("wg_t", [IC, 128, 2048], F32, kind="ExternalInput").ap()
    wu_t = nc.dram_tensor("wu_t", [IC, 128, 2048], F32, kind="ExternalInput").ap()
    wd_t = nc.dram_tensor("wd_t", [16, 128, IC, 128], F32, kind="ExternalInput").ap()
    b_til = nc.dram_tensor("b_til", [128, 48], F32, kind="ExternalInput").ap()
    bo_til = nc.dram_tensor("bo_til", [128, 16], F32, kind="ExternalInput").ap()
    gs_til = nc.dram_tensor("gs_til", [128, IC], F32, kind="ExternalInput").ap()
    us_til = nc.dram_tensor("us_til", [128, IC], F32, kind="ExternalInput").ap()
    ds_til = nc.dram_tensor("ds_til", [128, 16], F32, kind="ExternalInput").ap()
    sflag = nc.dram_tensor("sflag", [128, 1], F32, kind="ExternalInput").ap()
    cflag = nc.dram_tensor("cflag", [128, 1], F32, kind="ExternalInput").ap()
    out_t = nc.dram_tensor("out_t", [128, 16, T], F32, kind="ExternalOutput").ap()
    cc_in = nc.dram_tensor("cc_in", [128, 2048], BF16, kind="Internal").ap()
    cc_out = nc.dram_tensor("cc_out", [128, 2048], BF16, kind="Internal").ap()

    dq_np, dk_np, dmt_np, lam_c = _consts()
    ident_d = nc.inline_tensor(np.eye(128, dtype=BF), "identc").ap()
    onescol_d = nc.inline_tensor(np.ones((128, 1), dtype=BF), "onescolc").ap()
    onesrow_d = nc.inline_tensor(np.ones((1, 128), dtype=np.float32), "onesrowc").ap()
    lamI_np = np.zeros((128, H * 128), dtype=np.float64)
    for h in range(H):
        lamI_np[:, h * 128:(h + 1) * 128][np.arange(128), np.arange(128)] = lam_c[h]
    lamI_d = nc.inline_tensor(np.ascontiguousarray(lamI_np.astype(BF)), "lamic").ap()
    dq_d = nc.inline_tensor(dq_np, "dqc").ap()
    dk_d = nc.inline_tensor(dk_np, "dkc").ap()
    dmt_d = nc.inline_tensor(dmt_np, "dmtc").ap()

    tap_d = {}
    for name, shape in (
        ("kf_ctx", [128, 16, T]), ("vf_ctx", [128, 16, T]), ("S", [128, 2048]),
        ("qf", [128, 16, T]), ("kf", [128, 16, T]), ("vf", [128, 16, T]),
        ("attn", [128, 16, T]), ("r", [128, 16, T]), ("x2", [128, 16, T]),
        ("hdn1", [128, ICH, T]),
    ):
        if name in taps:
            tap_d[name] = nc.dram_tensor("tap_" + name, shape, F32,
                                         kind="ExternalOutput").ap()

    ident = nc.alloc_sbuf_tensor("ident_sb", [128, 128], BF16).ap()
    ones_col = nc.alloc_sbuf_tensor("onescol_sb", [128, 1], BF16).ap()
    ones_row = nc.alloc_sbuf_tensor("onesrow_sb", [1, 128], F32).ap()
    dq_all = nc.alloc_sbuf_tensor("dq_sb", [128, 2048], BF16).ap()
    dk_all = nc.alloc_sbuf_tensor("dk_sb", [128, 2048], BF16).ap()
    dmt_all = nc.alloc_sbuf_tensor("dmt_sb", [128, 2048], BF16).ap()
    lamI_sb = nc.alloc_sbuf_tensor("lamI_sb", [128, 2048], BF16).ap()
    b_sb = nc.alloc_sbuf_tensor("b_sb", [128, 48], F32).ap()
    bo_sb = nc.alloc_sbuf_tensor("bo_sb", [128, 16], F32).ap()
    gs_sb = nc.alloc_sbuf_tensor("gs_sb", [128, IC], F32).ap()
    us_sb = nc.alloc_sbuf_tensor("us_sb", [128, IC], F32).ap()
    ds_sb = nc.alloc_sbuf_tensor("ds_sb", [128, 16], F32).ap()
    flag_sb = nc.alloc_sbuf_tensor("flag_sb", [128, 1], F32).ap()
    cflag_sb = nc.alloc_sbuf_tensor("cflag_sb", [128, 1], F32).ap()
    s1_row = nc.alloc_sbuf_tensor("s1_row", [1, T], F32).ap()
    s2_row = nc.alloc_sbuf_tensor("s2_row", [1, T], F32).ap()
    s3_row = nc.alloc_sbuf_tensor("s3_row", [1, T], F32).ap()

    KB = 1024
    dyn = ((229344 - nc.sbuf_bytes_remaining) + 255) // 256 * 256
    assert dyn + 175 * KB <= 229344, dyn

    def at(name, shape, dtype, off):
        return nc.alloc_sbuf_tensor_at(name, shape, dtype, offset=dyn + off).ap()

    xb = at("xb", [128, 16, T], BF16, 0)
    attn_f = at("attn_f", [128, 16, T], BF16, 0)
    hdn_a = at("hdn_a", [128, 16, T], BF16, 0)
    kf = at("kf", [128, 16, T], BF16, 32 * KB)
    vf = at("vf", [128, 16, T], BF16, 64 * KB)
    r_f32 = at("r_f32", [128, 16, T], F32, 32 * KB)
    cstage = at("cstage", [128, 2048], BF16, 32 * KB)
    recv_sb = at("recv_sb", [128, 2048], BF16, 64 * KB)
    qf = at("qf", [128, 16, T], BF16, 96 * KB)
    x2b = at("x2b", [128, 16, T], BF16, 96 * KB)
    WO = 128 * KB
    wpA = at("wpA", [128, 3, 2048], BF16, WO + 0)
    outbuf = at("outbuf", [128, 8, T], F32, 96 * KB)
    wpB = at("wpB", [128, 2, 2048], BF16, WO + 8 * KB)
    xres = at("xres", [128, 2, T], F32, WO + 8 * KB)
    wpD = at("wpD", [128, 2, ICH * 128], BF16, WO + 16 * KB)
    S_f32 = at("S_f32", [128, 2048], F32, WO + 16 * KB)
    ntmp = at("ntmp", [128, 2, T], F32, WO + 16 * KB)
    XO = WO + 27 * KB
    ks2 = at("ks2", [128, 2, 2048], BF16, WO + 0)
    vtm2 = at("vtm2", [128, 2, 2048], BF16, WO + 8 * KB)
    sm_sb = at("sm_sb", [128, 2048], BF16, XO + 8 * KB)
    qs_sb = at("qs_sb", [128, 2048], BF16, XO + 12 * KB)
    S_bf = at("S_bf", [128, 2048], BF16, XO + 16 * KB)
    hdn_b = at("hdn_b", [128, 6, T], BF16, XO + 0)
    rb_buf = at("rb_buf", [128, 2, T], BF16, XO + 12 * KB)
    sq_buf = at("sq_buf", [128, 2, T], BF16, XO + 16 * KB)
    silu_buf = at("silu_buf", [128, 2, T], BF16, XO + 12 * KB)
    mbc_sb = at("mbc_sb", [128, 2, T], F32, XO + 12 * KB)

    def hdn_tile(i):
        sl = i % ICH
        return hdn_a[:, sl, :] if sl < 16 else hdn_b[:, sl - 16, :]

    s_in = _Sem(nc, "s_in")
    s_w = _Sem(nc, "s_w")
    s_pe = _Sem(nc, "s_pe")
    s_ac = _Sem(nc, "s_ac")
    s_ve = _Sem(nc, "s_ve")
    s_st = _Sem(nc, "s_st")
    s_gp = _Sem(nc, "s_gp")

    stA = ExitStack()
    stT = ExitStack()
    psA = stA.enter_context(nc.psum_tensor("psA", [128, 2048], F32)).ap()
    psT = stT.enter_context(nc.psum_tensor("psT", [128, 2048], BF16)).ap()
    psT2 = stT.enter_context(nc.psum_tensor("psT2", [128, 2048], BF16)).ap()

    mk = {}

    def W(eng, *pairs):
        for s, v in pairs:
            if v:
                eng.wait_ge(s.h, v)

    with nc.Block() as blk:
        @blk.sync
        def _(sy):
            for dst, src in ((ident, ident_d), (ones_row, onesrow_d),
                             (dq_all, dq_d), (dk_all, dk_d), (dmt_all, dmt_d),
                             (b_sb, b_til), (bo_sb, bo_til), (gs_sb, gs_til),
                             (us_sb, us_til), (ds_sb, ds_til), (flag_sb, sflag),
                             (cflag_sb, cflag), (ones_col, onescol_d),
                             (lamI_sb, lamI_d)):
                s_in.inc(sy.dma_start(out=dst[:, :], in_=src[:, :]), 16)
        consts_done = s_in.n

        @blk.gpsimd
        def _(g):
            s_w.inc(g.dma_start(out=xb[:, :, :], in_=x_own[:, :, :]), 16)
        xb_w = s_w.n

        @blk.vector
        def _(v):
            W(v, (s_in, consts_done))
            mk["s_zero"] = s_ve.inc(v.memset(S_f32[:, :], 0.0))

        @blk.tensor
        def _(te):
            W(te, (s_in, consts_done))
            for i in range(64):
                te.transpose(psT[:, (i % 16) * 128:(i % 16) * 128 + 128],
                             ident[:, :], ident[:, :])

        def proj_pass(n_pieces, piece_src, rhs3, dest_of, bias_of,
                      pe_first=(), gp_first=()):
            mm_at, ev_at = {}, {}
            for mi in range(n_pieces):
                @blk.gpsimd
                def _(g, mi=mi):
                    if mi == 0:
                        W(g, *gp_first)
                    if mi >= 3:
                        W(g, (s_pe, mm_at[mi - 3]))
                    s_w.inc(g.dma_start(out=wpA[:, mi % 3, :],
                                        in_=piece_src(mi)), 16)
                w_done = s_w.n

                @blk.tensor
                def _(te, mi=mi, w_done=w_done):
                    W(te, (s_w, w_done))
                    if mi == 0:
                        W(te, (s_in, consts_done), *pe_first)
                    if mi >= 2:
                        W(te, (s_ac, ev_at[mi - 2]))
                    half = mi % 2
                    for kc2 in range(KC):
                        w_ap = wpA[:, mi % 3, kc2 * 128:(kc2 + 1) * 128]
                        te.matmul(psA[:, half * 1024:half * 1024 + 512],
                                  w_ap, rhs3[:, kc2, 0:512],
                                  start=(kc2 == 0), stop=False)
                        mm = te.matmul(psA[:, half * 1024 + 512:(half + 1) * 1024],
                                       w_ap, rhs3[:, kc2, 512:1024],
                                       start=(kc2 == 0), stop=(kc2 == KC - 1))
                    mm_at[mi] = s_pe.inc(mm)

                @blk.scalar
                def _(s, mi=mi):
                    W(s, (s_pe, mm_at[mi]))
                    half = mi % 2
                    ins = s.activation(dest_of(mi),
                                       psA[:, half * 1024:(half + 1) * 1024],
                                       AF.Identity, bias=bias_of(mi), scale=1.0)
                    ev_at[mi] = s_ac.inc(ins)
            return mm_at, ev_at

        def scanA1(c):
            cs = slice(c * 128, (c + 1) * 128)

            @blk.tensor
            def _(te):
                W(te, mk.get("psT_free", (s_ve, 0)), mk.pop("kvf_ready", (s_ac, 0)))
                for h in range(H):
                    t = te.transpose(psT[:, h * 128:(h + 1) * 128], kf[:, h, cs],
                                     ident[:, :])
                mk["tk"] = s_pe.inc(t)
                W(te, mk.get("psT2_free", (s_ac, 0)))
                for h in range(H):
                    t = te.transpose(psT2[:, h * 128:(h + 1) * 128], vf[:, h, cs],
                                     ident[:, :])
                mk["tv"] = s_pe.inc(t)

        def scanA2(c):
            cs = slice(c * 128, (c + 1) * 128)
            p = c % 2
            slot_free = mk.get("ut1_c%d" % (c - 2), 0)

            @blk.tensor
            def _(te):
                for gg in range(2):
                    W(te, mk.get("psA%d_free" % gg, (s_ve, 0)))
                    for h in range(gg * 8, gg * 8 + 8):
                        t = te.matmul(psA[:, h * 128:(h + 1) * 128], kf[:, h, cs],
                                      qf[:, h, cs], start=(h % 4 == 0),
                                      stop=(h % 4 == 3), skip_group_check=True)
                    mk["sc%d" % gg] = s_pe.inc(t)

            @blk.vector
            def _(v, c=c, p=p, slot_free=slot_free):
                W(v, mk.pop("qf_ready", (s_ac, 0)))
                W(v, (s_pe, mk["tk"]), (s_pe, slot_free))
                mk["ks"] = s_ve.inc(
                    v.tensor_tensor(ks2[:, p, :], psT[:, :], dk_all[:, :], ALU.mult))
                v.tensor_tensor(qs_sb.rearrange("p (h i) -> p h i", h=H),
                                qf[:, :, cs],
                                dq_all.rearrange("p (h i) -> p h i", h=H),
                                ALU.mult)
                for gg in range(2):
                    gs_ = slice(gg * 1024, (gg + 1) * 1024)
                    W(v, (s_pe, mk["sc%d" % gg]))
                    mk["sm%d" % gg] = s_ve.inc(
                        v.tensor_tensor(sm_sb[:, gs_], psA[:, gs_],
                                        dmt_all[:, gs_], ALU.mult))
            mk["psT_free"] = (s_ve, mk["ks"])

            @blk.scalar
            def _(s, c=c, p=p, slot_free=slot_free):
                if c == 0:
                    W(s, (s_ve, mk["s_zero"]))
                    mk["sbf"] = s_ac.inc(s.copy(S_bf[:, :], S_f32[:, :]))
                W(s, (s_pe, mk["tv"]), (s_pe, slot_free))
                mk["vtm"] = s_ac.inc(s.copy(vtm2[:, p, :], psT2[:, :]))
            mk["psT2_free"] = (s_ac, mk["vtm"])



        def scanB_at(c):
            cs = slice(c * 128, (c + 1) * 128)
            p = c % 2
            for gg in range(2):
                @blk.tensor
                def _(te, gg=gg):
                    W(te, (s_ve, mk["sm%d" % gg]))
                    if gg == 0:
                        W(te, (s_ac, mk["sbf"]), (s_ac, mk["vtm"]))
                    for h in range(gg * 8, gg * 8 + 8):
                        hs = slice(h * 128, (h + 1) * 128)
                        te.matmul(psA[:, hs], S_bf[:, hs], qs_sb[:, hs],
                                  start=(h % 4 == 0), stop=False,
                                  skip_group_check=True)
                        t = te.matmul(psA[:, hs], vtm2[:, p, hs], sm_sb[:, hs],
                                      start=False, stop=(h % 4 == 3),
                                      skip_group_check=True)
                    mk["at%d" % gg] = s_pe.inc(t)

                @blk.scalar
                def _(s, gg=gg):
                    W(s, (s_pe, mk["at%d" % gg]))
                    mk["atev%d" % gg] = s_ac.inc(
                        s.copy(attn_f[:, gg * 8:(gg + 1) * 8, cs],
                               psA[:, gg * 1024:(gg + 1) * 1024]
                               .rearrange("p (h i) -> p h i", h=8)))

        def scanB_ut(c):
            p = c % 2
            for gg in range(2):
                @blk.tensor
                def _(te, gg=gg):
                    W(te, (s_ac, mk["atev%d" % gg]))
                    for h in range(gg * 8, gg * 8 + 8):
                        hs = slice(h * 128, (h + 1) * 128)
                        te.matmul(psA[:, hs], ks2[:, p, hs], vtm2[:, p, hs],
                                  start=(h % 4 == 0), stop=False,
                                  skip_group_check=True)
                        t = te.matmul(psA[:, hs], lamI_sb[:, hs], S_bf[:, hs],
                                      start=False, stop=(h % 4 == 3),
                                      skip_group_check=True)
                    mk["ut%d" % gg] = s_pe.inc(t)
            mk["ut1_c%d" % c] = mk["ut1"]

            for gg in range(2):
                @blk.scalar
                def _(s, gg=gg, c=c):
                    W(s, (s_pe, mk["ut%d" % gg]))
                    gsl = slice(gg * 1024, (gg + 1) * 1024)
                    mk["supd%d" % gg] = s_ac.inc(s.copy(S_f32[:, gsl], psA[:, gsl]))
                    if gg == 1 and c < NCH - 1:
                        mk["sbf"] = s_ac.inc(s.copy(S_bf[:, :], S_f32[:, :]))
                mk["psA%d_free" % gg] = (s_ac, mk["supd%d" % gg])

        def own_dest(mi):
            return (qf, kf, vf)[mi // 16][:, mi % 16, :]
        mm3, ev3 = proj_pass(
            48, lambda mi: wqkv_t[mi, :, :], xb,
            own_dest, lambda mi: b_sb[:, mi:mi + 1])
        seg3_end_pe = s_pe.n

        for tname, tsrc in (("qf", qf), ("kf", kf), ("vf", vf)):
            if tname in taps:
                @blk.gpsimd
                def _(g, tname=tname, tsrc=tsrc):
                    W(g, (s_ac, ev3[47]))
                    s_st.inc(g.dma_start(out=tap_d[tname][:, :, :],
                                         in_=tsrc[:, :, :]), 16)

        mk["psT_free"] = (s_ve, 0)
        mk["psA0_free"] = (s_ac, ev3[47])
        mk["psA1_free"] = (s_ac, ev3[47])
        mk["kvf_ready"] = (s_ac, ev3[47])
        mk["qf_ready"] = (s_ac, ev3[47])
        scanA1(0)
        scanA2(0)
        for c in range(NCH):
            scanB_at(c)
            if c + 1 < NCH:
                scanA1(c + 1)
            scanB_ut(c)
            if c + 1 < NCH:
                scanA2(c + 1)
        scan_end_pe = mk["ut1"]
        attn_all = mk["supd1"]

        s_cc = _Sem(nc, "s_cc")

        @blk.vector
        def _(v):
            W(v, (s_ac, mk["supd1"]))
            mk["cst"] = s_ve.inc(
                v.tensor_scalar_mul(cstage[:, :], S_f32[:, :], cflag_sb[:, 0:1]))

        @blk.sync
        def _(sy):
            W(sy, (s_ve, mk["cst"]))
            mk["cc_st"] = s_st.inc(sy.dma_start(out=cc_in[:, :], in_=cstage[:, :]), 16)

        @blk.gpsimd
        def _(g):
            W(g, (s_st, mk["cc_st"]))
            if use_cc:
                s_cc.inc(g.collective_compute(
                    "AllReduce", ALU.add,
                    replica_groups=[[0, 1], [2, 3], [4, 5], [6, 7]],
                    ins=[cc_in.opt()], outs=[cc_out.opt()]))
            else:
                s_cc.inc(g.dma_start(out=cc_out[:, :], in_=cc_in[:, :]), 16)

        @blk.sync
        def _(sy):
            W(sy, (s_cc, s_cc.n))
            mk["rc_ld"] = s_in.inc(sy.dma_start(out=recv_sb[:, :], in_=cc_out[:, :]), 16)

        @blk.vector
        def _(v):
            W(v, (s_in, mk["rc_ld"]))
            mk["sprev"] = s_ve.inc(
                v.tensor_scalar_mul(S_bf[:, :], recv_sb[:, :], flag_sb[:, 0:1]))

        for l in range(NCH):
            ls = slice(l * 128, (l + 1) * 128)

            @blk.vector
            def _(v, l=l):
                mk["cqs"] = s_ve.inc(
                    v.tensor_tensor(qs_sb.rearrange("p (h i) -> p h i", h=H),
                                    qf[:, :, ls],
                                    dq_all.rearrange("p (h i) -> p h i", h=H),
                                    ALU.mult))

            for gg in range(2):
                @blk.tensor
                def _(te, l=l, gg=gg):
                    if gg == 0:
                        W(te, (s_ve, mk["cqs"]))
                    for h in range(gg * 8, gg * 8 + 8):
                        hs = slice(h * 128, (h + 1) * 128)
                        t = te.matmul(psA[:, hs], S_bf[:, hs], qs_sb[:, hs],
                                      start=(h % 4 == 0), stop=(h % 4 == 3),
                                      skip_group_check=True)
                    mk["cmm%d" % gg] = s_pe.inc(t)

                @blk.vector
                def _(v, l=l, gg=gg):
                    W(v, (s_pe, mk["cmm%d" % gg]))
                    for h in range(gg * 8, gg * 8 + 8):
                        hs = slice(h * 128, (h + 1) * 128)
                        t = v.scalar_tensor_tensor(
                            attn_f[:, h, ls], psA[:, hs], float(lam_c[h] ** l),
                            attn_f[:, h, ls], ALU.mult, ALU.add)
                    mk["corr%d" % gg] = s_ve.inc(t)
        attn_all = mk["corr1"]

        if "attn" in taps:
            @blk.gpsimd
            def _(g):
                W(g, (s_ve, attn_all))
                s_st.inc(g.dma_start(out=tap_d["attn"][:, :, :], in_=attn_f[:, :, :]), 16)

        stT.close()
        stS = ExitStack()
        psS1 = stS.enter_context(nc.psum_tensor("psS1", [1, T], F32)).ap()
        psS2 = stS.enter_context(nc.psum_tensor("psS2", [1, T], F32)).ap()

        rb_at, sq_at, st1_at = {}, {}, {}
        r_at, omm_at = {}, {}

        def stats_mm(j, ps1, ps2):
            @blk.tensor
            def _(te, j=j):
                W(te, (s_ve, rb_at[j]))
                te.matmul(ps1[0:1, 0:512], ones_col[:, :], rb_buf[:, j % 2, 0:512],
                          start=(j == 0), stop=(j == 15), skip_group_check=True)
                te.matmul(ps1[0:1, 512:1024], ones_col[:, :], rb_buf[:, j % 2, 512:1024],
                          start=(j == 0), stop=(j == 15), skip_group_check=True)
                W(te, (s_ac, sq_at[j]))
                te.matmul(ps2[0:1, 0:512], ones_col[:, :], sq_buf[:, j % 2, 0:512],
                          start=(j == 0), stop=(j == 15), skip_group_check=True)
                t = te.matmul(ps2[0:1, 512:1024], ones_col[:, :],
                              sq_buf[:, j % 2, 512:1024],
                              start=(j == 0), stop=(j == 15), skip_group_check=True)
                st1_at[j] = s_pe.inc(t)

        def ln_rows_and_bcast(ps1, ps2, psBC):
            inv = 1.0 / DM
            @blk.vector
            def _(v):
                W(v, (s_pe, st1_at[15]))
                v.tensor_scalar_mul(s1_row[0:1, :], ps1[0:1, :], inv)
                v.tensor_scalar_mul(s2_row[0:1, :], ps2[0:1, :], inv)
                v.tensor_tensor(s3_row[0:1, :], s1_row[0:1, :], s1_row[0:1, :], ALU.mult)
                v.tensor_tensor(s2_row[0:1, :], s2_row[0:1, :], s3_row[0:1, :],
                                ALU.subtract)
                t = v.tensor_scalar_add(s2_row[0:1, :], s2_row[0:1, :], EPS)
                mk["var"] = s_ve.inc(t)

            @blk.scalar
            def _(s):
                W(s, (s_ve, mk["var"]))
                s.activation(s3_row[0:1, :], s2_row[0:1, :], AF.Ln)
                mk["rstd"] = s_ac.inc(
                    s.activation(s3_row[0:1, :], s3_row[0:1, :], AF.Exp, scale=-0.5))

            @blk.tensor
            def _(te):
                W(te, (s_ac, mk["rstd"]))
                for hh in range(2):
                    te.matmul(psBC[:, hh * 512:hh * 512 + 512], ones_row[0:1, :],
                              s1_row[0:1, hh * 512:hh * 512 + 512],
                              start=True, stop=True, skip_group_check=True)
                    t = te.matmul(psBC[:, 1024 + hh * 512:1536 + hh * 512],
                                  ones_row[0:1, :],
                                  s3_row[0:1, hh * 512:hh * 512 + 512],
                                  start=True, stop=True, skip_group_check=True)
                mk["bc"] = s_pe.inc(t)

            @blk.vector
            def _(v):
                W(v, (s_pe, mk["bc"]))
                v.tensor_copy(mbc_sb[:, 0, :], psBC[:, 0:1024])
                t = v.tensor_copy(mbc_sb[:, 1, :], psBC[:, 1024:2048])
                mk["mbc"] = s_ve.inc(t)

        for j in range(16):
            @blk.gpsimd
            def _(g, j=j):
                if j <= 1:
                    W(g, (s_pe, scan_end_pe))
                if j >= 2:
                    W(g, (s_pe, omm_at[j - 2]))
                s_w.inc(g.dma_start(out=wpA[:, j % 2, :], in_=wo_t[j, :, :]), 16)
            wj = s_w.n

            @blk.sync
            def _(sy, j=j):
                if j >= 2:
                    W(sy, (s_ve, r_at[j - 2]))
                s_in.inc(sy.dma_start(out=xres[:, j % 2, :], in_=x_own[:, j, :]), 16)
            xj = s_in.n

            @blk.tensor
            def _(te, j=j, wj=wj):
                W(te, (s_w, wj))
                if j == 0:
                    W(te, (s_ve, attn_all))
                if j >= 2:
                    W(te, (s_ve, r_at[j - 2]))
                half = j % 2
                for kc2 in range(KC):
                    w_ap = wpA[:, half, kc2 * 128:(kc2 + 1) * 128]
                    te.matmul(psA[:, half * 1024:half * 1024 + 512], w_ap,
                              attn_f[:, kc2, 0:512], start=(kc2 == 0), stop=False)
                    t = te.matmul(psA[:, half * 1024 + 512:(half + 1) * 1024], w_ap,
                                  attn_f[:, kc2, 512:1024], start=(kc2 == 0),
                                  stop=(kc2 == KC - 1))
                omm_at[j] = s_pe.inc(t)

            @blk.vector
            def _(v, j=j, xj=xj):
                W(v, (s_pe, omm_at[j]), (s_in, xj))
                half = j % 2
                v.scalar_tensor_tensor(r_f32[:, j, :],
                                       psA[:, half * 1024:(half + 1) * 1024],
                                       bo_sb[:, j:j + 1], xres[:, half, :],
                                       ALU.add, ALU.add)
                t = v.tensor_copy(rb_buf[:, j % 2, :], r_f32[:, j, :])
                r_at[j] = rb_at[j] = s_ve.inc(t)

            @blk.scalar
            def _(s, j=j):
                W(s, (s_ve, rb_at[j]))
                sq_at[j] = s_ac.inc(
                    s.activation(sq_buf[:, j % 2, :], rb_buf[:, j % 2, :], AF.Square))
            if j >= 1:
                stats_mm(j - 1, psS1, psS2)
            if j == 15:
                stats_mm(15, psS1, psS2)

        if "r" in taps:
            @blk.sync
            def _(sy):
                W(sy, (s_ve, r_at[15]))
                s_st.inc(sy.dma_start(out=tap_d["r"][:, :, :], in_=r_f32[:, :, :]), 16)

        ln_rows_and_bcast(psS1, psS2, psA)
        bc1 = mk["bc"]

        pre_w = {}

        @blk.gpsimd
        def _(g):
            W(g, (s_pe, omm_at[15]), (s_ve, r_at[15]))
            for i in (0, 1):
                s_w.inc(g.dma_start(out=wpA[:, i % 2, :], in_=wg_t[i, :, :]), 16)
                s_w.inc(g.dma_start(out=wpB[:, i % 2, :], in_=wu_t[i, :, :]), 16)
                pre_w[i] = s_w.n

        x2b_rdy = {}
        for j in range(13):
            @blk.vector
            def _(v, j=j):
                if j == 0:
                    W(v, (s_ve, mk["mbc"]))
                v.tensor_tensor(ntmp[:, 0, :], r_f32[:, j, :],
                                mbc_sb[:, 0, :], ALU.subtract)
                x2b_rdy[j] = ("v", s_ve.inc(
                    v.tensor_tensor(x2b[:, j, :], ntmp[:, 0, :],
                                    mbc_sb[:, 1, :], ALU.mult)))
        for j in range(13, 16):
            @blk.gpsimd
            def _(g, j=j):
                if j == 13:
                    g.wait_ge(s_ve.h, mk["mbc"])
                g.tensor_tensor(ntmp[:, 1, :], r_f32[:, j, :],
                                mbc_sb[:, 0, :], ALU.subtract)
                x2b_rdy[j] = ("g", s_gp.inc(
                    g.tensor_tensor(x2b[:, j, :], ntmp[:, 1, :],
                                    mbc_sb[:, 1, :], ALU.mult)))
        ln1_doneV = x2b_rdy[12][1]
        ln1_doneP = x2b_rdy[15][1]
        ln1_done = ln1_doneV

        if "x2" in taps:
            @blk.gpsimd
            def _(g):
                g.wait_ge(s_ve.h, ln1_doneV)
                g.wait_ge(s_gp.h, ln1_doneP)
                s_st.inc(g.dma_start(out=tap_d["x2"][:, :, :], in_=x2b[:, :, :]), 16)

        stS.close()
        stA.close()

        gu_mm_at, hdn_at = {}, {}
        dmm_at, r2_at = {}, {}

        def gate_up(psG, i_lo, i_hi, gp_first=(), pe_first=(), preload=None):
            for i in range(i_lo, i_hi):
                io = i - i_lo

                if preload is not None and io < 2:
                    wi = preload[io]
                else:
                    @blk.gpsimd
                    def _(g, i=i, io=io):
                        if io == 0:
                            W(g, *gp_first)
                        if io >= 2:
                            W(g, (s_pe, gu_mm_at[i - 2]))
                        s_w.inc(g.dma_start(out=wpA[:, i % 2, :], in_=wg_t[i, :, :]), 16)
                        s_w.inc(g.dma_start(out=wpB[:, i % 2, :], in_=wu_t[i, :, :]), 16)
                    wi = s_w.n

                @blk.tensor
                def _(te, i=i, io=io, wi=wi):
                    W(te, (s_w, wi))
                    if io == 0:
                        W(te, *pe_first)
                    if io >= 2:
                        W(te, (s_ve, hdn_at[i - 2]))
                    q = (i % 2) * 2048
                    for kc2 in range(KC):
                        if io == 0 and i == 0 and kc2 in x2b_rdy:
                            eng, vv = x2b_rdy[kc2]
                            W(te, (s_ve if eng == "v" else s_gp, vv))
                        wg_ap = wpA[:, i % 2, kc2 * 128:(kc2 + 1) * 128]
                        wu_ap = wpB[:, i % 2, kc2 * 128:(kc2 + 1) * 128]
                        te.matmul(psG[:, q:q + 512], wg_ap, x2b[:, kc2, 0:512],
                                  start=(kc2 == 0), stop=False)
                        te.matmul(psG[:, q + 512:q + 1024], wg_ap, x2b[:, kc2, 512:1024],
                                  start=(kc2 == 0), stop=(kc2 == KC - 1))
                        te.matmul(psG[:, q + 1024:q + 1536], wu_ap, x2b[:, kc2, 0:512],
                                  start=(kc2 == 0), stop=False)
                        t = te.matmul(psG[:, q + 1536:q + 2048], wu_ap,
                                      x2b[:, kc2, 512:1024],
                                      start=(kc2 == 0), stop=(kc2 == KC - 1))
                    gu_mm_at[i] = s_pe.inc(t)

                @blk.scalar
                def _(s, i=i):
                    W(s, (s_pe, gu_mm_at[i]))
                    q = (i % 2) * 2048
                    mk["sil"] = s_ac.inc(
                        s.activation(silu_buf[:, i % 2, :], psG[:, q:q + 1024],
                                     AF.Silu, scale=gs_sb[:, i:i + 1]))

                @blk.vector
                def _(v, i=i):
                    W(v, (s_ac, mk["sil"]))
                    q = (i % 2) * 2048
                    hdn_at[i] = s_ve.inc(
                        v.scalar_tensor_tensor(hdn_tile(i), psG[:, q + 1024:q + 2048],
                                               us_sb[:, i:i + 1], silu_buf[:, i % 2, :],
                                               ALU.mult, ALU.mult))

        def down_pass(psD, ih, ps1=None, ps2=None, gp_first=()):
            for j in range(16):
                @blk.gpsimd
                def _(g, j=j):
                    if j == 0:
                        W(g, *gp_first)
                    if j >= 2:
                        W(g, (s_pe, dmm_at[(ih, j - 2)]))
                    s_w.inc(g.dma_start(
                        out=wpD[:, j % 2, :],
                        in_=wd_t[j, :, ih * ICH:(ih + 1) * ICH, :]), 16)
                wj = s_w.n

                @blk.tensor
                def _(te, j=j, wj=wj):
                    W(te, (s_w, wj))
                    if j == 0:
                        W(te, (s_ve, hdn_at[ih * ICH + ICH - 1]))
                    if j >= 2:
                        W(te, (s_ve, r2_at[(ih, j - 2)]))
                    half = j % 2
                    for ii in range(ICH):
                        w_ap = wpD[:, half, ii * 128:(ii + 1) * 128]
                        ht = hdn_tile(ih * ICH + ii)
                        te.matmul(psD[:, half * 1024:half * 1024 + 512], w_ap,
                                  ht[:, 0:512], start=(ii == 0), stop=False)
                        t = te.matmul(psD[:, half * 1024 + 512:(half + 1) * 1024], w_ap,
                                      ht[:, 512:1024], start=(ii == 0),
                                      stop=(ii == ICH - 1))
                    dmm_at[(ih, j)] = s_pe.inc(t)

                @blk.vector
                def _(v, j=j):
                    W(v, (s_pe, dmm_at[(ih, j)]))
                    half = j % 2
                    if ih == 0:
                        t = v.scalar_tensor_tensor(
                            r_f32[:, j, :], psD[:, half * 1024:(half + 1) * 1024],
                            ds_sb[:, j:j + 1], x2b[:, j, :], ALU.mult, ALU.add)
                    else:
                        v.scalar_tensor_tensor(
                            r_f32[:, j, :], psD[:, half * 1024:(half + 1) * 1024],
                            ds_sb[:, j:j + 1], r_f32[:, j, :], ALU.mult, ALU.add)
                        t = v.tensor_copy(rb_buf[:, j % 2, :], r_f32[:, j, :])
                    r2_at[(ih, j)] = s_ve.inc(t)

                if ih == 1:
                    rb_at[j] = r2_at[(1, j)]

                    @blk.scalar
                    def _(s, j=j):
                        W(s, (s_ve, rb_at[j]))
                        sq_at[j] = s_ac.inc(
                            s.activation(sq_buf[:, j % 2, :], rb_buf[:, j % 2, :],
                                         AF.Square))
                    if j >= 1:
                        stats_mm(j - 1, ps1, ps2)
                    if j == 15:
                        stats_mm(15, ps1, ps2)

        st6 = ExitStack()
        psG = st6.enter_context(nc.psum_tensor("psG", [128, 4096], F32)).ap()
        gate_up(psG, 0, ICH, preload=pre_w)
        if "hdn1" in taps:
            @blk.gpsimd
            def _(g):
                W(g, (s_ve, hdn_at[ICH - 1]))
                for i in range(ICH):
                    s_st.inc(g.dma_start(out=tap_d["hdn1"][:, i, :], in_=hdn_tile(i)), 16)

        st6.close()
        st7 = ExitStack()
        psD = st7.enter_context(nc.psum_tensor("psD", [128, 2048], F32)).ap()
        down_pass(psD, 0, gp_first=((s_ve, ln1_doneV), (s_gp, ln1_doneP)))
        seg7_r2 = r2_at[(0, 15)]

        st7.close()
        st8 = ExitStack()
        psG2 = st8.enter_context(nc.psum_tensor("psG2", [128, 4096], F32)).ap()
        gate_up(psG2, ICH, IC,
                gp_first=((s_pe, gu_mm_at[ICH - 1]),),
                pe_first=((s_ve, seg7_r2),))

        st8.close()
        st9 = ExitStack()
        psD2 = st9.enter_context(nc.psum_tensor("psD2", [128, 2048], F32)).ap()
        psS1b = st9.enter_context(nc.psum_tensor("psS1b", [1, T], F32)).ap()
        psS2b = st9.enter_context(nc.psum_tensor("psS2b", [1, T], F32)).ap()
        down_pass(psD2, 1, ps1=psS1b, ps2=psS2b,
                  gp_first=((s_pe, dmm_at[(0, 15)]),))

        ln_rows_and_bcast(psS1b, psS2b, psD2)
        bc2 = mk["bc"]

        out_st = {}
        onrm = {}
        for j in range(13):
            @blk.vector
            def _(v, j=j):
                if j == 0:
                    W(v, (s_ve, mk["mbc"]))
                if j >= 4:
                    W(v, (s_st, out_st[j - 4]))
                v.tensor_tensor(ntmp[:, 0, :], r_f32[:, j, :],
                                mbc_sb[:, 0, :], ALU.subtract)
                onrm[j] = ("v", s_ve.inc(
                    v.tensor_tensor(outbuf[:, j % 4, :], ntmp[:, 0, :],
                                    mbc_sb[:, 1, :], ALU.mult)))

            @blk.sync
            def _(sy, j=j):
                W(sy, (s_ve, onrm[j][1]))
                out_st[j] = s_st.inc(
                    sy.dma_start(out=out_t[:, j, :], in_=outbuf[:, j % 4, :]), 16)

        for j in range(13, 16):
            @blk.gpsimd
            def _(g, j=j):
                if j == 13:
                    g.wait_ge(s_ve.h, mk["mbc"])
                if j >= 15:
                    g.wait_ge(s_st.h, out_st[j - 4])
                g.tensor_tensor(ntmp[:, 1, :], r_f32[:, j, :],
                                mbc_sb[:, 0, :], ALU.subtract)
                onrm[j] = ("g", s_gp.inc(
                    g.tensor_tensor(outbuf[:, 4 + j % 4, :], ntmp[:, 1, :],
                                    mbc_sb[:, 1, :], ALU.mult)))

            @blk.sync
            def _(sy, j=j):
                W(sy, (s_gp, onrm[j][1]))
                out_st[j] = s_st.inc(
                    sy.dma_start(out=out_t[:, j, :], in_=outbuf[:, 4 + j % 4, :]), 16)

        @blk.sync
        def _(sy):
            W(sy, (s_st, s_st.n))
        st9.close()

    return nc


def _prepare(inputs):
    wqkv_t = _tile_w(np.asarray(inputs["L0_attn_w_qkv"], np.float32))
    wo_t = _tile_w(np.asarray(inputs["L0_attn_w_out"], np.float32))
    wg_t = _tile_w(np.asarray(inputs["L0_ffn_gate_w"], np.float32))
    wu_t = _tile_w(np.asarray(inputs["L0_ffn_up_w"], np.float32))
    wd_t = _tile_wd(np.asarray(inputs["L0_ffn_down_w"], np.float32))
    b_til = np.ascontiguousarray(
        np.asarray(inputs["L0_attn_b_qkv"], np.float32).reshape(48, 128).T)
    bo_til = np.ascontiguousarray(
        np.asarray(inputs["L0_attn_b_out"], np.float32).reshape(16, 128).T)
    gs_til = np.ascontiguousarray(
        np.asarray(inputs["L0_ffn_gate_s"], np.float32).reshape(IC, 128).T)
    us_til = np.ascontiguousarray(
        np.asarray(inputs["L0_ffn_up_s"], np.float32).reshape(IC, 128).T)
    ds_til = np.ascontiguousarray(
        np.asarray(inputs["L0_ffn_down_s"], np.float32).reshape(16, 128).T)
    return dict(wqkv_t=wqkv_t, wo_t=wo_t, wg_t=wg_t, wu_t=wu_t, wd_t=wd_t,
                b_til=b_til, bo_til=bo_til, gs_til=gs_til, us_til=us_til,
                ds_til=ds_til)


def make_in_maps(inputs):
    shared = _prepare(inputs)
    x = np.asarray(inputs["L0_attn_x"], np.float32)
    flag0 = np.zeros((128, 1), np.float32)
    flag1 = np.ones((128, 1), np.float32)
    in_maps = []
    for c in range(8):
        b, half = c // 2, c % 2
        xT = np.ascontiguousarray(x[b].T)
        m = dict(shared)
        m["x_own"] = _tile_x(xT[:, half * T:(half + 1) * T])
        m["sflag"] = flag1 if half == 1 else flag0
        m["cflag"] = flag0 if half == 1 else flag1
        in_maps.append(m)
    return in_maps


def kernel(**inputs):
    nc = build_program()
    in_maps = make_in_maps(inputs)
    res = run_bass_kernel_spmd(nc, in_maps, core_ids=list(range(8)))
    out = np.empty((BATCH, SEQ, DM), np.float32)
    for c in range(8):
        b, half = c // 2, c % 2
        ot = res.results[c]["out_t"]
        out[b, half * T:(half + 1) * T, :] = ot.transpose(1, 0, 2).reshape(DM, T).T
    return out


# version 15
# speedup vs baseline: 1.0944x; 1.0944x over previous
import math
from contextlib import ExitStack

import numpy as np
import ml_dtypes
import concourse.bass as bass
import concourse.mybir as mybir
from concourse.bass_utils import run_bass_kernel_spmd

F32 = mybir.dt.float32
BF16 = mybir.dt.bfloat16
AF = mybir.ActivationFunctionType
ALU = mybir.AluOpType

H = 16
DM = 2048
T = 1024
NCH = 8
KC = 16
INTER = 5632
IC = 44
ICH = 22
BATCH = 4
SEQ = 2048
EPS = 1e-5

BF = ml_dtypes.bfloat16


def _lam64():
    start = 2.0 ** (-2.0 ** (-(math.log2(H) - 3)))
    slopes = np.array([start ** (i + 1) for i in range(H)], dtype=np.float64)
    return np.exp(-slopes)


def _consts():
    lam = _lam64()
    i_idx = np.arange(128, dtype=np.float64)
    dq = (lam[:, None] ** (i_idx[None, :] + 1.0)).reshape(1, H * 128)
    dq_all = np.broadcast_to(dq, (128, H * 128))
    dk = lam[None, :] ** (127.0 - i_idx[:, None])
    dk_all = np.repeat(dk, 128, axis=1)
    diff = i_idx[None, :] - i_idx[:, None]
    dm_t = np.concatenate(
        [np.where(diff >= 0, lam[h] ** np.maximum(diff, 0.0), 0.0)
         for h in range(H)], axis=1)
    lam_c = lam ** 128.0
    return (np.ascontiguousarray(dq_all.astype(BF)),
            np.ascontiguousarray(dk_all.astype(BF)),
            np.ascontiguousarray(dm_t.astype(BF)),
            [float(v) for v in lam_c])


def _tile_w(w):
    M, K = w.shape
    nm, nk = M // 128, K // 128
    t = w.reshape(nm, 128, nk, 128)
    t = np.ascontiguousarray(t.transpose(0, 3, 2, 1))
    return t.reshape(nm, 128, K)


def _tile_wd(w_down):
    t = w_down.reshape(16, 128, IC, 128)
    return np.ascontiguousarray(t.transpose(0, 3, 2, 1))


def _tile_x(xT_half):
    t = xT_half.reshape(16, 128, T).transpose(1, 0, 2)
    return np.ascontiguousarray(t)


class _Sem:
    def __init__(self, nc, name):
        self.h = nc.alloc_semaphore(name)
        self.n = 0

    def inc(self, inst, k=1):
        inst.then_inc(self.h, k)
        self.n += k
        return self.n


def build_program(taps=(), use_cc=True):
    taps = set(taps)
    nc = bass.Bass("TRN2", target_bir_lowering=False)

    x_own = nc.dram_tensor("x_own", [128, 16, T], F32, kind="ExternalInput").ap()
    wqkv_t = nc.dram_tensor("wqkv_t", [48, 128, 2048], F32, kind="ExternalInput").ap()
    wo_t = nc.dram_tensor("wo_t", [16, 128, 2048], F32, kind="ExternalInput").ap()
    wg_t = nc.dram_tensor("wg_t", [IC, 128, 2048], F32, kind="ExternalInput").ap()
    wu_t = nc.dram_tensor("wu_t", [IC, 128, 2048], F32, kind="ExternalInput").ap()
    wd_t = nc.dram_tensor("wd_t", [16, 128, IC, 128], F32, kind="ExternalInput").ap()
    b_til = nc.dram_tensor("b_til", [128, 48], F32, kind="ExternalInput").ap()
    bo_til = nc.dram_tensor("bo_til", [128, 16], F32, kind="ExternalInput").ap()
    gs_til = nc.dram_tensor("gs_til", [128, IC], F32, kind="ExternalInput").ap()
    us_til = nc.dram_tensor("us_til", [128, IC], F32, kind="ExternalInput").ap()
    ds_til = nc.dram_tensor("ds_til", [128, 16], F32, kind="ExternalInput").ap()
    sflag = nc.dram_tensor("sflag", [128, 1], F32, kind="ExternalInput").ap()
    cflag = nc.dram_tensor("cflag", [128, 1], F32, kind="ExternalInput").ap()
    out_t = nc.dram_tensor("out_t", [128, 16, T], F32, kind="ExternalOutput").ap()
    cc_in = nc.dram_tensor("cc_in", [128, 2048], BF16, kind="Internal").ap()
    cc_out = nc.dram_tensor("cc_out", [128, 2048], BF16, kind="Internal").ap()

    dq_np, dk_np, dmt_np, lam_c = _consts()
    ident_d = nc.inline_tensor(np.eye(128, dtype=BF), "identc").ap()
    onescol_d = nc.inline_tensor(np.ones((128, 1), dtype=BF), "onescolc").ap()
    onesrow_d = nc.inline_tensor(np.ones((1, 128), dtype=np.float32), "onesrowc").ap()
    lamI_np = np.zeros((128, H * 128), dtype=np.float64)
    for h in range(H):
        lamI_np[:, h * 128:(h + 1) * 128][np.arange(128), np.arange(128)] = lam_c[h]
    lamI_d = nc.inline_tensor(np.ascontiguousarray(lamI_np.astype(BF)), "lamic").ap()
    lamc_np = np.repeat(np.array(lam_c, dtype=np.float64)[None, :], 128, 0)
    lamc_np = np.repeat(lamc_np, 128, 1).astype(BF)
    lamc_d = nc.inline_tensor(np.ascontiguousarray(lamc_np), "lamcc").ap()
    dq_d = nc.inline_tensor(dq_np, "dqc").ap()
    dk_d = nc.inline_tensor(dk_np, "dkc").ap()
    dmt_d = nc.inline_tensor(dmt_np, "dmtc").ap()

    tap_d = {}
    for name, shape in (
        ("kf_ctx", [128, 16, T]), ("vf_ctx", [128, 16, T]), ("S", [128, 2048]),
        ("qf", [128, 16, T]), ("kf", [128, 16, T]), ("vf", [128, 16, T]),
        ("attn", [128, 16, T]), ("r", [128, 16, T]), ("x2", [128, 16, T]),
        ("hdn1", [128, ICH, T]),
    ):
        if name in taps:
            tap_d[name] = nc.dram_tensor("tap_" + name, shape, F32,
                                         kind="ExternalOutput").ap()

    ident = nc.alloc_sbuf_tensor("ident_sb", [128, 128], BF16).ap()
    ones_col = nc.alloc_sbuf_tensor("onescol_sb", [128, 1], BF16).ap()
    ones_row = nc.alloc_sbuf_tensor("onesrow_sb", [1, 128], F32).ap()
    dq_all = nc.alloc_sbuf_tensor("dq_sb", [128, 2048], BF16).ap()
    dk_all = nc.alloc_sbuf_tensor("dk_sb", [128, 2048], BF16).ap()
    dmt_all = nc.alloc_sbuf_tensor("dmt_sb", [128, 2048], BF16).ap()
    lamI_sb = nc.alloc_sbuf_tensor("lamI_sb", [128, 2048], BF16).ap()
    b_sb = nc.alloc_sbuf_tensor("b_sb", [128, 48], F32).ap()
    bo_sb = nc.alloc_sbuf_tensor("bo_sb", [128, 16], F32).ap()
    gs_sb = nc.alloc_sbuf_tensor("gs_sb", [128, IC], F32).ap()
    us_sb = nc.alloc_sbuf_tensor("us_sb", [128, IC], F32).ap()
    ds_sb = nc.alloc_sbuf_tensor("ds_sb", [128, 16], F32).ap()
    flag_sb = nc.alloc_sbuf_tensor("flag_sb", [128, 1], F32).ap()
    cflag_sb = nc.alloc_sbuf_tensor("cflag_sb", [128, 1], F32).ap()
    s1_row = nc.alloc_sbuf_tensor("s1_row", [1, T], F32).ap()
    s2_row = nc.alloc_sbuf_tensor("s2_row", [1, T], F32).ap()
    s3_row = nc.alloc_sbuf_tensor("s3_row", [1, T], F32).ap()

    KB = 1024
    dyn = ((229344 - nc.sbuf_bytes_remaining) + 255) // 256 * 256
    assert dyn + 175 * KB <= 229344, dyn

    def at(name, shape, dtype, off):
        return nc.alloc_sbuf_tensor_at(name, shape, dtype, offset=dyn + off).ap()

    xb = at("xb", [128, 16, T], BF16, 0)
    attn_f = at("attn_f", [128, 16, T], BF16, 0)
    hdn_a = at("hdn_a", [128, 16, T], BF16, 0)
    kf = at("kf", [128, 16, T], BF16, 32 * KB)
    vf = at("vf", [128, 16, T], BF16, 64 * KB)
    r_f32 = at("r_f32", [128, 16, T], F32, 32 * KB)
    cstage = at("cstage", [128, 2048], BF16, 32 * KB)
    recv_sb = at("recv_sb", [128, 2048], BF16, 64 * KB)
    qf = at("qf", [128, 16, T], BF16, 96 * KB)
    x2b = at("x2b", [128, 16, T], BF16, 96 * KB)
    WO = 128 * KB
    wpA = at("wpA", [128, 3, 2048], BF16, WO + 0)
    outbuf = at("outbuf", [128, 8, T], F32, 96 * KB)
    wpB = at("wpB", [128, 2, 2048], BF16, WO + 8 * KB)
    xres = at("xres", [128, 2, T], F32, WO + 8 * KB)
    wpD = at("wpD", [128, 2, ICH * 128], BF16, WO + 16 * KB)
    S_f32 = at("S_f32", [128, 2048], F32, WO + 16 * KB)
    ntmp = at("ntmp", [128, 2, T], F32, WO + 16 * KB)
    XO = WO + 27 * KB
    ks2 = at("ks2", [128, 2, 2048], BF16, WO + 0)
    dql = at("dql", [128, 2048], BF16, WO + 8 * KB)
    lamc_sb = at("lamc_sb", [128, 2048], BF16, XO + 0)
    vtm2 = at("vtm2", [128, 2, 2048], BF16, WO + 8 * KB)
    sm_sb = at("sm_sb", [128, 2048], BF16, XO + 8 * KB)
    qs_sb = at("qs_sb", [128, 2048], BF16, XO + 12 * KB)
    S_bf = at("S_bf", [128, 2048], BF16, XO + 16 * KB)
    hdn_b = at("hdn_b", [128, 6, T], BF16, XO + 0)
    rb_buf = at("rb_buf", [128, 2, T], BF16, XO + 12 * KB)
    sq_buf = at("sq_buf", [128, 2, T], BF16, XO + 16 * KB)
    silu_buf = at("silu_buf", [128, 2, T], BF16, XO + 12 * KB)
    mbc_sb = at("mbc_sb", [128, 2, T], F32, XO + 12 * KB)

    def hdn_tile(i):
        sl = i % ICH
        return hdn_a[:, sl, :] if sl < 16 else hdn_b[:, sl - 16, :]

    s_in = _Sem(nc, "s_in")
    s_w = _Sem(nc, "s_w")
    s_pe = _Sem(nc, "s_pe")
    s_ac = _Sem(nc, "s_ac")
    s_ve = _Sem(nc, "s_ve")
    s_st = _Sem(nc, "s_st")
    s_gp = _Sem(nc, "s_gp")

    stA = ExitStack()
    stT = ExitStack()
    psA = stA.enter_context(nc.psum_tensor("psA", [128, 2048], F32)).ap()
    psT = stT.enter_context(nc.psum_tensor("psT", [128, 2048], BF16)).ap()
    psT2 = stT.enter_context(nc.psum_tensor("psT2", [128, 2048], BF16)).ap()

    mk = {}

    def W(eng, *pairs):
        for s, v in pairs:
            if v:
                eng.wait_ge(s.h, v)

    with nc.Block() as blk:
        @blk.sync
        def _(sy):
            for dst, src in ((ident, ident_d), (ones_row, onesrow_d),
                             (dq_all, dq_d), (dk_all, dk_d), (dmt_all, dmt_d),
                             (b_sb, b_til), (bo_sb, bo_til), (gs_sb, gs_til),
                             (us_sb, us_til), (ds_sb, ds_til), (flag_sb, sflag),
                             (cflag_sb, cflag), (ones_col, onescol_d),
                             (lamI_sb, lamI_d), (lamc_sb, lamc_d)):
                s_in.inc(sy.dma_start(out=dst[:, :], in_=src[:, :]), 16)
        consts_done = s_in.n

        @blk.gpsimd
        def _(g):
            s_w.inc(g.dma_start(out=xb[:, :, :], in_=x_own[:, :, :]), 16)
        xb_w = s_w.n

        @blk.vector
        def _(v):
            W(v, (s_in, consts_done))
            mk["s_zero"] = s_ve.inc(v.memset(S_f32[:, :], 0.0))


        def proj_pass(n_pieces, piece_src, rhs3, dest_of, bias_of,
                      pe_first=(), gp_first=()):
            mm_at, ev_at = {}, {}
            for mi in range(n_pieces):
                @blk.gpsimd
                def _(g, mi=mi):
                    if mi == 0:
                        W(g, *gp_first)
                    if mi >= 3:
                        W(g, (s_pe, mm_at[mi - 3]))
                    s_w.inc(g.dma_start(out=wpA[:, mi % 3, :],
                                        in_=piece_src(mi)), 16)
                w_done = s_w.n

                @blk.tensor
                def _(te, mi=mi, w_done=w_done):
                    W(te, (s_w, w_done))
                    if mi == 0:
                        W(te, (s_in, consts_done), *pe_first)
                    if mi >= 2:
                        W(te, (s_ac, ev_at[mi - 2]))
                    half = mi % 2
                    for kc2 in range(KC):
                        w_ap = wpA[:, mi % 3, kc2 * 128:(kc2 + 1) * 128]
                        te.matmul(psA[:, half * 1024:half * 1024 + 512],
                                  w_ap, rhs3[:, kc2, 0:512],
                                  start=(kc2 == 0), stop=False)
                        mm = te.matmul(psA[:, half * 1024 + 512:(half + 1) * 1024],
                                       w_ap, rhs3[:, kc2, 512:1024],
                                       start=(kc2 == 0), stop=(kc2 == KC - 1))
                    mm_at[mi] = s_pe.inc(mm)

                @blk.scalar
                def _(s, mi=mi):
                    W(s, (s_pe, mm_at[mi]))
                    half = mi % 2
                    ins = s.activation(dest_of(mi),
                                       psA[:, half * 1024:(half + 1) * 1024],
                                       AF.Identity, bias=bias_of(mi), scale=1.0)
                    ev_at[mi] = s_ac.inc(ins)
            return mm_at, ev_at

        def scanA1(c):
            cs = slice(c * 128, (c + 1) * 128)

            @blk.tensor
            def _(te):
                W(te, mk.get("psT_free", (s_ve, 0)), mk.pop("kvf_ready", (s_ac, 0)))
                for h in range(H):
                    t = te.transpose(psT[:, h * 128:(h + 1) * 128], kf[:, h, cs],
                                     ident[:, :])
                mk["tk"] = s_pe.inc(t)
                W(te, mk.get("psT2_free", (s_ac, 0)))
                for h in range(H):
                    t = te.transpose(psT2[:, h * 128:(h + 1) * 128], vf[:, h, cs],
                                     ident[:, :])
                mk["tv"] = s_pe.inc(t)

        def scanA2(c):
            cs = slice(c * 128, (c + 1) * 128)
            p = c % 2
            slot_free = mk.get("ut1_c%d" % (c - 2), 0)

            @blk.tensor
            def _(te):
                for gg in range(2):
                    W(te, mk.get("psA%d_free" % gg, (s_ve, 0)))
                    for h in range(gg * 8, gg * 8 + 8):
                        t = te.matmul(psA[:, h * 128:(h + 1) * 128], kf[:, h, cs],
                                      qf[:, h, cs], start=(h % 4 == 0),
                                      stop=(h % 4 == 3), skip_group_check=True)
                    mk["sc%d" % gg] = s_pe.inc(t)

            @blk.vector
            def _(v, c=c, p=p, slot_free=slot_free):
                W(v, mk.pop("qf_ready", (s_ac, 0)))
                W(v, (s_pe, mk["tk"]), (s_pe, slot_free))
                mk["ks"] = s_ve.inc(
                    v.tensor_tensor(ks2[:, p, :], psT[:, :], dk_all[:, :], ALU.mult))
                v.tensor_tensor(qs_sb.rearrange("p (h i) -> p h i", h=H),
                                qf[:, :, cs],
                                dq_all.rearrange("p (h i) -> p h i", h=H),
                                ALU.mult)
                for gg in range(2):
                    gs_ = slice(gg * 1024, (gg + 1) * 1024)
                    W(v, (s_pe, mk["sc%d" % gg]))
                    mk["sm%d" % gg] = s_ve.inc(
                        v.tensor_tensor(sm_sb[:, gs_], psA[:, gs_],
                                        dmt_all[:, gs_], ALU.mult))
            mk["psT_free"] = (s_ve, mk["ks"])

            @blk.scalar
            def _(s, c=c, p=p, slot_free=slot_free):
                if c == 0:
                    W(s, (s_ve, mk["s_zero"]))
                    mk["sbf"] = s_ac.inc(s.copy(S_bf[:, :], S_f32[:, :]))
                W(s, (s_pe, mk["tv"]), (s_pe, slot_free))
                mk["vtm"] = s_ac.inc(s.copy(vtm2[:, p, :], psT2[:, :]))
            mk["psT2_free"] = (s_ac, mk["vtm"])



        def scanB_at(c):
            cs = slice(c * 128, (c + 1) * 128)
            p = c % 2
            for gg in range(2):
                @blk.tensor
                def _(te, gg=gg):
                    W(te, (s_ve, mk["sm%d" % gg]))
                    if gg == 0:
                        W(te, (s_ac, mk["sbf"]), (s_ac, mk["vtm"]))
                    for h in range(gg * 8, gg * 8 + 8):
                        hs = slice(h * 128, (h + 1) * 128)
                        te.matmul(psA[:, hs], S_bf[:, hs], qs_sb[:, hs],
                                  start=(h % 4 == 0), stop=False,
                                  skip_group_check=True)
                        t = te.matmul(psA[:, hs], vtm2[:, p, hs], sm_sb[:, hs],
                                      start=False, stop=(h % 4 == 3),
                                      skip_group_check=True)
                    mk["at%d" % gg] = s_pe.inc(t)

                @blk.scalar
                def _(s, gg=gg):
                    W(s, (s_pe, mk["at%d" % gg]))
                    mk["atev%d" % gg] = s_ac.inc(
                        s.copy(attn_f[:, gg * 8:(gg + 1) * 8, cs],
                               psA[:, gg * 1024:(gg + 1) * 1024]
                               .rearrange("p (h i) -> p h i", h=8)))

        def scanB_ut(c):
            p = c % 2
            for gg in range(2):
                @blk.tensor
                def _(te, gg=gg):
                    W(te, (s_ac, mk["atev%d" % gg]))
                    for h in range(gg * 8, gg * 8 + 8):
                        hs = slice(h * 128, (h + 1) * 128)
                        te.matmul(psA[:, hs], ks2[:, p, hs], vtm2[:, p, hs],
                                  start=(h % 4 == 0), stop=False,
                                  skip_group_check=True)
                        t = te.matmul(psA[:, hs], lamI_sb[:, hs], S_bf[:, hs],
                                      start=False, stop=(h % 4 == 3),
                                      skip_group_check=True)
                    mk["ut%d" % gg] = s_pe.inc(t)
            mk["ut1_c%d" % c] = mk["ut1"]

            for gg in range(2):
                @blk.scalar
                def _(s, gg=gg, c=c):
                    W(s, (s_pe, mk["ut%d" % gg]))
                    gsl = slice(gg * 1024, (gg + 1) * 1024)
                    mk["supd%d" % gg] = s_ac.inc(s.copy(S_f32[:, gsl], psA[:, gsl]))
                    if gg == 1 and c < NCH - 1:
                        mk["sbf"] = s_ac.inc(s.copy(S_bf[:, :], S_f32[:, :]))
                mk["psA%d_free" % gg] = (s_ac, mk["supd%d" % gg])

        def own_dest(mi):
            return (qf, kf, vf)[mi // 16][:, mi % 16, :]
        mm3, ev3 = proj_pass(
            48, lambda mi: wqkv_t[mi, :, :], xb,
            own_dest, lambda mi: b_sb[:, mi:mi + 1])
        seg3_end_pe = s_pe.n

        for tname, tsrc in (("qf", qf), ("kf", kf), ("vf", vf)):
            if tname in taps:
                @blk.gpsimd
                def _(g, tname=tname, tsrc=tsrc):
                    W(g, (s_ac, ev3[47]))
                    s_st.inc(g.dma_start(out=tap_d[tname][:, :, :],
                                         in_=tsrc[:, :, :]), 16)

        mk["psT_free"] = (s_ve, 0)
        mk["psA0_free"] = (s_ac, ev3[47])
        mk["psA1_free"] = (s_ac, ev3[47])
        mk["kvf_ready"] = (s_ac, ev3[47])
        mk["qf_ready"] = (s_ac, ev3[47])
        scanA1(0)
        scanA2(0)
        for c in range(NCH):
            scanB_at(c)
            if c + 1 < NCH:
                scanA1(c + 1)
            scanB_ut(c)
            if c + 1 < NCH:
                scanA2(c + 1)
        scan_end_pe = mk["ut1"]
        attn_all = mk["supd1"]

        s_cc = _Sem(nc, "s_cc")

        @blk.vector
        def _(v):
            W(v, (s_ac, mk["supd1"]))
            mk["cst"] = s_ve.inc(
                v.tensor_scalar_mul(cstage[:, :], S_f32[:, :], cflag_sb[:, 0:1]))

        @blk.sync
        def _(sy):
            W(sy, (s_ve, mk["cst"]))
            mk["cc_st"] = s_st.inc(sy.dma_start(out=cc_in[:, :], in_=cstage[:, :]), 16)

        @blk.gpsimd
        def _(g):
            W(g, (s_st, mk["cc_st"]))
            if use_cc:
                s_cc.inc(g.collective_compute(
                    "AllReduce", ALU.add,
                    replica_groups=[[0, 1], [2, 3], [4, 5], [6, 7]],
                    ins=[cc_in.opt()], outs=[cc_out.opt()]))
            else:
                s_cc.inc(g.dma_start(out=cc_out[:, :], in_=cc_in[:, :]), 16)

        @blk.sync
        def _(sy):
            W(sy, (s_cc, s_cc.n))
            mk["rc_ld"] = s_in.inc(sy.dma_start(out=recv_sb[:, :], in_=cc_out[:, :]), 16)

        @blk.vector
        def _(v):
            W(v, (s_in, mk["rc_ld"]))
            mk["sprev"] = s_ve.inc(
                v.tensor_scalar_mul(S_bf[:, :], recv_sb[:, :], flag_sb[:, 0:1]))

        for l in range(NCH):
            ls = slice(l * 128, (l + 1) * 128)

            @blk.vector
            def _(v, l=l):
                if l == 0:
                    v.tensor_copy(dql[:, :], dq_all[:, :])
                else:
                    v.tensor_tensor(dql[:, :], dql[:, :], lamc_sb[:, :], ALU.mult)
                mk["cqs"] = s_ve.inc(
                    v.tensor_tensor(qs_sb.rearrange("p (h i) -> p h i", h=H),
                                    qf[:, :, ls],
                                    dql.rearrange("p (h i) -> p h i", h=H),
                                    ALU.mult))

            for gg in range(2):
                @blk.tensor
                def _(te, l=l, gg=gg):
                    if gg == 0:
                        W(te, (s_ve, mk["cqs"]))
                        if l == 0:
                            W(te, (s_ve, mk["sprev"]))
                    for h in range(gg * 8, gg * 8 + 8):
                        hs = slice(h * 128, (h + 1) * 128)
                        t = te.matmul(psA[:, hs], S_bf[:, hs], qs_sb[:, hs],
                                      start=(h % 4 == 0), stop=(h % 4 == 3),
                                      skip_group_check=True)
                    mk["cmm%d" % gg] = s_pe.inc(t)

                @blk.vector
                def _(v, l=l, gg=gg):
                    W(v, (s_pe, mk["cmm%d" % gg]))
                    gsl = slice(gg * 1024, (gg + 1) * 1024)
                    t = v.scalar_tensor_tensor(
                        attn_f[:, gg * 8:(gg + 1) * 8, ls],
                        psA[:, gsl].rearrange("p (h i) -> p h i", h=8), 1.0,
                        attn_f[:, gg * 8:(gg + 1) * 8, ls],
                        ALU.mult, ALU.add)
                    mk["corr%d" % gg] = s_ve.inc(t)
        attn_all = mk["corr1"]

        if "attn" in taps:
            @blk.gpsimd
            def _(g):
                W(g, (s_ve, attn_all))
                s_st.inc(g.dma_start(out=tap_d["attn"][:, :, :], in_=attn_f[:, :, :]), 16)

        stT.close()
        stS = ExitStack()
        psS1 = stS.enter_context(nc.psum_tensor("psS1", [1, T], F32)).ap()
        psS2 = stS.enter_context(nc.psum_tensor("psS2", [1, T], F32)).ap()

        rb_at, sq_at, st1_at = {}, {}, {}
        r_at, omm_at = {}, {}

        def stats_mm(j, ps1, ps2):
            @blk.tensor
            def _(te, j=j):
                W(te, (s_ve, rb_at[j]))
                te.matmul(ps1[0:1, 0:512], ones_col[:, :], rb_buf[:, j % 2, 0:512],
                          start=(j == 0), stop=(j == 15), skip_group_check=True)
                te.matmul(ps1[0:1, 512:1024], ones_col[:, :], rb_buf[:, j % 2, 512:1024],
                          start=(j == 0), stop=(j == 15), skip_group_check=True)
                W(te, (s_ac, sq_at[j]))
                te.matmul(ps2[0:1, 0:512], ones_col[:, :], sq_buf[:, j % 2, 0:512],
                          start=(j == 0), stop=(j == 15), skip_group_check=True)
                t = te.matmul(ps2[0:1, 512:1024], ones_col[:, :],
                              sq_buf[:, j % 2, 512:1024],
                              start=(j == 0), stop=(j == 15), skip_group_check=True)
                st1_at[j] = s_pe.inc(t)

        def ln_rows_and_bcast(ps1, ps2, psBC):
            inv = 1.0 / DM
            @blk.vector
            def _(v):
                W(v, (s_pe, st1_at[15]))
                v.tensor_scalar_mul(s1_row[0:1, :], ps1[0:1, :], inv)
                v.tensor_scalar_mul(s2_row[0:1, :], ps2[0:1, :], inv)
                v.tensor_tensor(s3_row[0:1, :], s1_row[0:1, :], s1_row[0:1, :], ALU.mult)
                v.tensor_tensor(s2_row[0:1, :], s2_row[0:1, :], s3_row[0:1, :],
                                ALU.subtract)
                t = v.tensor_scalar_add(s2_row[0:1, :], s2_row[0:1, :], EPS)
                mk["var"] = s_ve.inc(t)

            @blk.scalar
            def _(s):
                W(s, (s_ve, mk["var"]))
                s.activation(s3_row[0:1, :], s2_row[0:1, :], AF.Ln)
                mk["rstd"] = s_ac.inc(
                    s.activation(s3_row[0:1, :], s3_row[0:1, :], AF.Exp, scale=-0.5))

            @blk.tensor
            def _(te):
                W(te, (s_ac, mk["rstd"]))
                for hh in range(2):
                    te.matmul(psBC[:, hh * 512:hh * 512 + 512], ones_row[0:1, :],
                              s1_row[0:1, hh * 512:hh * 512 + 512],
                              start=True, stop=True, skip_group_check=True)
                    t = te.matmul(psBC[:, 1024 + hh * 512:1536 + hh * 512],
                                  ones_row[0:1, :],
                                  s3_row[0:1, hh * 512:hh * 512 + 512],
                                  start=True, stop=True, skip_group_check=True)
                mk["bc"] = s_pe.inc(t)

            @blk.vector
            def _(v):
                W(v, (s_pe, mk["bc"]))
                v.tensor_copy(mbc_sb[:, 0, :], psBC[:, 0:1024])
                t = v.tensor_copy(mbc_sb[:, 1, :], psBC[:, 1024:2048])
                mk["mbc"] = s_ve.inc(t)

        for j in range(16):
            @blk.gpsimd
            def _(g, j=j):
                if j <= 1:
                    W(g, (s_pe, scan_end_pe))
                if j >= 2:
                    W(g, (s_pe, omm_at[j - 2]))
                s_w.inc(g.dma_start(out=wpA[:, j % 2, :], in_=wo_t[j, :, :]), 16)
            wj = s_w.n

            @blk.sync
            def _(sy, j=j):
                if j <= 1:
                    W(sy, (s_ve, attn_all))
                if j >= 2:
                    W(sy, (s_ve, r_at[j - 2]))
                s_in.inc(sy.dma_start(out=xres[:, j % 2, :], in_=x_own[:, j, :]), 16)
            xj = s_in.n

            @blk.tensor
            def _(te, j=j, wj=wj):
                W(te, (s_w, wj))
                if j == 0:
                    W(te, (s_ve, attn_all))
                if j >= 2:
                    W(te, (s_ve, r_at[j - 2]))
                half = j % 2
                for kc2 in range(KC):
                    w_ap = wpA[:, half, kc2 * 128:(kc2 + 1) * 128]
                    te.matmul(psA[:, half * 1024:half * 1024 + 512], w_ap,
                              attn_f[:, kc2, 0:512], start=(kc2 == 0), stop=False)
                    t = te.matmul(psA[:, half * 1024 + 512:(half + 1) * 1024], w_ap,
                                  attn_f[:, kc2, 512:1024], start=(kc2 == 0),
                                  stop=(kc2 == KC - 1))
                omm_at[j] = s_pe.inc(t)

            @blk.vector
            def _(v, j=j, xj=xj):
                W(v, (s_pe, omm_at[j]), (s_in, xj))
                half = j % 2
                v.scalar_tensor_tensor(r_f32[:, j, :],
                                       psA[:, half * 1024:(half + 1) * 1024],
                                       bo_sb[:, j:j + 1], xres[:, half, :],
                                       ALU.add, ALU.add)
                t = v.tensor_copy(rb_buf[:, j % 2, :], r_f32[:, j, :])
                r_at[j] = rb_at[j] = s_ve.inc(t)

            @blk.scalar
            def _(s, j=j):
                W(s, (s_ve, rb_at[j]))
                sq_at[j] = s_ac.inc(
                    s.activation(sq_buf[:, j % 2, :], rb_buf[:, j % 2, :], AF.Square))
            if j >= 1:
                stats_mm(j - 1, psS1, psS2)
            if j == 15:
                stats_mm(15, psS1, psS2)

        if "r" in taps:
            @blk.sync
            def _(sy):
                W(sy, (s_ve, r_at[15]))
                s_st.inc(sy.dma_start(out=tap_d["r"][:, :, :], in_=r_f32[:, :, :]), 16)

        ln_rows_and_bcast(psS1, psS2, psA)
        bc1 = mk["bc"]

        pre_w = {}

        @blk.gpsimd
        def _(g):
            W(g, (s_pe, omm_at[15]), (s_ve, r_at[15]))
            for i in (0, 1):
                s_w.inc(g.dma_start(out=wpA[:, i % 2, :], in_=wg_t[i, :, :]), 16)
                s_w.inc(g.dma_start(out=wpB[:, i % 2, :], in_=wu_t[i, :, :]), 16)
                pre_w[i] = s_w.n

        x2b_rdy = {}
        for j in range(13):
            @blk.vector
            def _(v, j=j):
                if j == 0:
                    W(v, (s_ve, mk["mbc"]))
                v.tensor_tensor(ntmp[:, 0, :], r_f32[:, j, :],
                                mbc_sb[:, 0, :], ALU.subtract)
                x2b_rdy[j] = ("v", s_ve.inc(
                    v.tensor_tensor(x2b[:, j, :], ntmp[:, 0, :],
                                    mbc_sb[:, 1, :], ALU.mult)))
        for j in range(13, 16):
            @blk.gpsimd
            def _(g, j=j):
                if j == 13:
                    g.wait_ge(s_ve.h, mk["mbc"])
                g.tensor_tensor(ntmp[:, 1, :], r_f32[:, j, :],
                                mbc_sb[:, 0, :], ALU.subtract)
                x2b_rdy[j] = ("g", s_gp.inc(
                    g.tensor_tensor(x2b[:, j, :], ntmp[:, 1, :],
                                    mbc_sb[:, 1, :], ALU.mult)))
        ln1_doneV = x2b_rdy[12][1]
        ln1_doneP = x2b_rdy[15][1]
        ln1_done = ln1_doneV

        if "x2" in taps:
            @blk.gpsimd
            def _(g):
                g.wait_ge(s_ve.h, ln1_doneV)
                g.wait_ge(s_gp.h, ln1_doneP)
                s_st.inc(g.dma_start(out=tap_d["x2"][:, :, :], in_=x2b[:, :, :]), 16)

        stS.close()
        stA.close()

        gu_mm_at, hdn_at = {}, {}
        dmm_at, r2_at = {}, {}

        def gate_up(psG, i_lo, i_hi, gp_first=(), pe_first=(), preload=None):
            for i in range(i_lo, i_hi):
                io = i - i_lo

                if preload is not None and io < 2:
                    wi = preload[io]
                else:
                    @blk.gpsimd
                    def _(g, i=i, io=io):
                        if io == 0:
                            W(g, *gp_first)
                        if io >= 2:
                            W(g, (s_pe, gu_mm_at[i - 2]))
                        s_w.inc(g.dma_start(out=wpA[:, i % 2, :], in_=wg_t[i, :, :]), 16)
                        s_w.inc(g.dma_start(out=wpB[:, i % 2, :], in_=wu_t[i, :, :]), 16)
                    wi = s_w.n

                @blk.tensor
                def _(te, i=i, io=io, wi=wi):
                    W(te, (s_w, wi))
                    if io == 0:
                        W(te, *pe_first)
                    if io >= 2:
                        W(te, (s_ve, hdn_at[i - 2]))
                    q = (i % 2) * 2048
                    for kc2 in range(KC):
                        if io == 0 and i == 0 and kc2 in x2b_rdy:
                            eng, vv = x2b_rdy[kc2]
                            W(te, (s_ve if eng == "v" else s_gp, vv))
                        wg_ap = wpA[:, i % 2, kc2 * 128:(kc2 + 1) * 128]
                        wu_ap = wpB[:, i % 2, kc2 * 128:(kc2 + 1) * 128]
                        te.matmul(psG[:, q:q + 512], wg_ap, x2b[:, kc2, 0:512],
                                  start=(kc2 == 0), stop=False)
                        te.matmul(psG[:, q + 512:q + 1024], wg_ap, x2b[:, kc2, 512:1024],
                                  start=(kc2 == 0), stop=(kc2 == KC - 1))
                        te.matmul(psG[:, q + 1024:q + 1536], wu_ap, x2b[:, kc2, 0:512],
                                  start=(kc2 == 0), stop=False)
                        t = te.matmul(psG[:, q + 1536:q + 2048], wu_ap,
                                      x2b[:, kc2, 512:1024],
                                      start=(kc2 == 0), stop=(kc2 == KC - 1))
                    gu_mm_at[i] = s_pe.inc(t)

                @blk.scalar
                def _(s, i=i):
                    W(s, (s_pe, gu_mm_at[i]))
                    q = (i % 2) * 2048
                    mk["sil"] = s_ac.inc(
                        s.activation(silu_buf[:, i % 2, :], psG[:, q:q + 1024],
                                     AF.Silu, scale=gs_sb[:, i:i + 1]))

                @blk.vector
                def _(v, i=i):
                    W(v, (s_ac, mk["sil"]))
                    q = (i % 2) * 2048
                    hdn_at[i] = s_ve.inc(
                        v.scalar_tensor_tensor(hdn_tile(i), psG[:, q + 1024:q + 2048],
                                               us_sb[:, i:i + 1], silu_buf[:, i % 2, :],
                                               ALU.mult, ALU.mult))

        def down_pass(psD, ih, ps1=None, ps2=None, gp_first=()):
            for j in range(16):
                @blk.gpsimd
                def _(g, j=j):
                    if j == 0:
                        W(g, *gp_first)
                    if j >= 2:
                        W(g, (s_pe, dmm_at[(ih, j - 2)]))
                    s_w.inc(g.dma_start(
                        out=wpD[:, j % 2, :],
                        in_=wd_t[j, :, ih * ICH:(ih + 1) * ICH, :]), 16)
                wj = s_w.n

                @blk.tensor
                def _(te, j=j, wj=wj):
                    W(te, (s_w, wj))
                    if j == 0:
                        W(te, (s_ve, hdn_at[ih * ICH + ICH - 1]))
                    if j >= 2:
                        W(te, (s_ve, r2_at[(ih, j - 2)]))
                    half = j % 2
                    for ii in range(ICH):
                        w_ap = wpD[:, half, ii * 128:(ii + 1) * 128]
                        ht = hdn_tile(ih * ICH + ii)
                        te.matmul(psD[:, half * 1024:half * 1024 + 512], w_ap,
                                  ht[:, 0:512], start=(ii == 0), stop=False)
                        t = te.matmul(psD[:, half * 1024 + 512:(half + 1) * 1024], w_ap,
                                      ht[:, 512:1024], start=(ii == 0),
                                      stop=(ii == ICH - 1))
                    dmm_at[(ih, j)] = s_pe.inc(t)

                @blk.vector
                def _(v, j=j):
                    W(v, (s_pe, dmm_at[(ih, j)]))
                    half = j % 2
                    if ih == 0:
                        t = v.scalar_tensor_tensor(
                            r_f32[:, j, :], psD[:, half * 1024:(half + 1) * 1024],
                            ds_sb[:, j:j + 1], x2b[:, j, :], ALU.mult, ALU.add)
                    else:
                        v.scalar_tensor_tensor(
                            r_f32[:, j, :], psD[:, half * 1024:(half + 1) * 1024],
                            ds_sb[:, j:j + 1], r_f32[:, j, :], ALU.mult, ALU.add)
                        t = v.tensor_copy(rb_buf[:, j % 2, :], r_f32[:, j, :])
                    r2_at[(ih, j)] = s_ve.inc(t)

                if ih == 1:
                    rb_at[j] = r2_at[(1, j)]

                    @blk.scalar
                    def _(s, j=j):
                        W(s, (s_ve, rb_at[j]))
                        sq_at[j] = s_ac.inc(
                            s.activation(sq_buf[:, j % 2, :], rb_buf[:, j % 2, :],
                                         AF.Square))
                    if j >= 1:
                        stats_mm(j - 1, ps1, ps2)
                    if j == 15:
                        stats_mm(15, ps1, ps2)

        st6 = ExitStack()
        psG = st6.enter_context(nc.psum_tensor("psG", [128, 4096], F32)).ap()
        gate_up(psG, 0, ICH, preload=pre_w)
        if "hdn1" in taps:
            @blk.gpsimd
            def _(g):
                W(g, (s_ve, hdn_at[ICH - 1]))
                for i in range(ICH):
                    s_st.inc(g.dma_start(out=tap_d["hdn1"][:, i, :], in_=hdn_tile(i)), 16)

        st6.close()
        st7 = ExitStack()
        psD = st7.enter_context(nc.psum_tensor("psD", [128, 2048], F32)).ap()
        down_pass(psD, 0, gp_first=((s_ve, ln1_doneV), (s_gp, ln1_doneP)))
        seg7_r2 = r2_at[(0, 15)]

        st7.close()
        st8 = ExitStack()
        psG2 = st8.enter_context(nc.psum_tensor("psG2", [128, 4096], F32)).ap()
        gate_up(psG2, ICH, IC,
                gp_first=((s_pe, gu_mm_at[ICH - 1]),),
                pe_first=((s_ve, seg7_r2),))

        st8.close()
        st9 = ExitStack()
        psD2 = st9.enter_context(nc.psum_tensor("psD2", [128, 2048], F32)).ap()
        psS1b = st9.enter_context(nc.psum_tensor("psS1b", [1, T], F32)).ap()
        psS2b = st9.enter_context(nc.psum_tensor("psS2b", [1, T], F32)).ap()
        down_pass(psD2, 1, ps1=psS1b, ps2=psS2b,
                  gp_first=((s_pe, dmm_at[(0, 15)]),))

        ln_rows_and_bcast(psS1b, psS2b, psD2)
        bc2 = mk["bc"]

        out_st = {}
        onrm = {}
        for j in range(13):
            @blk.vector
            def _(v, j=j):
                if j == 0:
                    W(v, (s_ve, mk["mbc"]))
                if j >= 4:
                    W(v, (s_st, out_st[j - 4]))
                v.tensor_tensor(ntmp[:, 0, :], r_f32[:, j, :],
                                mbc_sb[:, 0, :], ALU.subtract)
                onrm[j] = ("v", s_ve.inc(
                    v.tensor_tensor(outbuf[:, j % 4, :], ntmp[:, 0, :],
                                    mbc_sb[:, 1, :], ALU.mult)))

            @blk.sync
            def _(sy, j=j):
                W(sy, (s_ve, onrm[j][1]))
                out_st[j] = s_st.inc(
                    sy.dma_start(out=out_t[:, j, :], in_=outbuf[:, j % 4, :]), 16)

        for j in range(13, 16):
            @blk.gpsimd
            def _(g, j=j):
                if j == 13:
                    g.wait_ge(s_ve.h, mk["mbc"])
                if j >= 15:
                    g.wait_ge(s_st.h, out_st[j - 4])
                g.tensor_tensor(ntmp[:, 1, :], r_f32[:, j, :],
                                mbc_sb[:, 0, :], ALU.subtract)
                onrm[j] = ("g", s_gp.inc(
                    g.tensor_tensor(outbuf[:, 4 + j % 4, :], ntmp[:, 1, :],
                                    mbc_sb[:, 1, :], ALU.mult)))

            @blk.sync
            def _(sy, j=j):
                W(sy, (s_gp, onrm[j][1]))
                out_st[j] = s_st.inc(
                    sy.dma_start(out=out_t[:, j, :], in_=outbuf[:, 4 + j % 4, :]), 16)

        @blk.sync
        def _(sy):
            W(sy, (s_st, s_st.n))
        st9.close()

    return nc


def _prepare(inputs):
    wqkv_t = _tile_w(np.asarray(inputs["L0_attn_w_qkv"], np.float32))
    wo_t = _tile_w(np.asarray(inputs["L0_attn_w_out"], np.float32))
    wg_t = _tile_w(np.asarray(inputs["L0_ffn_gate_w"], np.float32))
    wu_t = _tile_w(np.asarray(inputs["L0_ffn_up_w"], np.float32))
    wd_t = _tile_wd(np.asarray(inputs["L0_ffn_down_w"], np.float32))
    b_til = np.ascontiguousarray(
        np.asarray(inputs["L0_attn_b_qkv"], np.float32).reshape(48, 128).T)
    bo_til = np.ascontiguousarray(
        np.asarray(inputs["L0_attn_b_out"], np.float32).reshape(16, 128).T)
    gs_til = np.ascontiguousarray(
        np.asarray(inputs["L0_ffn_gate_s"], np.float32).reshape(IC, 128).T)
    us_til = np.ascontiguousarray(
        np.asarray(inputs["L0_ffn_up_s"], np.float32).reshape(IC, 128).T)
    ds_til = np.ascontiguousarray(
        np.asarray(inputs["L0_ffn_down_s"], np.float32).reshape(16, 128).T)
    return dict(wqkv_t=wqkv_t, wo_t=wo_t, wg_t=wg_t, wu_t=wu_t, wd_t=wd_t,
                b_til=b_til, bo_til=bo_til, gs_til=gs_til, us_til=us_til,
                ds_til=ds_til)


def make_in_maps(inputs):
    shared = _prepare(inputs)
    x = np.asarray(inputs["L0_attn_x"], np.float32)
    flag0 = np.zeros((128, 1), np.float32)
    flag1 = np.ones((128, 1), np.float32)
    in_maps = []
    for c in range(8):
        b, half = c // 2, c % 2
        xT = np.ascontiguousarray(x[b].T)
        m = dict(shared)
        m["x_own"] = _tile_x(xT[:, half * T:(half + 1) * T])
        m["sflag"] = flag1 if half == 1 else flag0
        m["cflag"] = flag0 if half == 1 else flag1
        in_maps.append(m)
    return in_maps


def kernel(**inputs):
    nc = build_program()
    in_maps = make_in_maps(inputs)
    res = run_bass_kernel_spmd(nc, in_maps, core_ids=list(range(8)))
    out = np.empty((BATCH, SEQ, DM), np.float32)
    for c in range(8):
        b, half = c // 2, c % 2
        ot = res.results[c]["out_t"]
        out[b, half * T:(half + 1) * T, :] = ot.transpose(1, 0, 2).reshape(DM, T).T
    return out


# version 16
# speedup vs baseline: 1.0991x; 1.0042x over previous
import math
from contextlib import ExitStack

import numpy as np
import ml_dtypes
import concourse.bass as bass
import concourse.mybir as mybir
from concourse.bass_utils import run_bass_kernel_spmd

F32 = mybir.dt.float32
BF16 = mybir.dt.bfloat16
AF = mybir.ActivationFunctionType
ALU = mybir.AluOpType

H = 16
DM = 2048
T = 1024
NCH = 8
KC = 16
INTER = 5632
IC = 44
ICH = 22
BATCH = 4
SEQ = 2048
EPS = 1e-5

BF = ml_dtypes.bfloat16


def _lam64():
    start = 2.0 ** (-2.0 ** (-(math.log2(H) - 3)))
    slopes = np.array([start ** (i + 1) for i in range(H)], dtype=np.float64)
    return np.exp(-slopes)


def _consts():
    lam = _lam64()
    i_idx = np.arange(128, dtype=np.float64)
    dq = (lam[:, None] ** (i_idx[None, :] + 1.0)).reshape(1, H * 128)
    dq_all = np.broadcast_to(dq, (128, H * 128))
    dk = lam[None, :] ** (127.0 - i_idx[:, None])
    dk_all = np.repeat(dk, 128, axis=1)
    diff = i_idx[None, :] - i_idx[:, None]
    dm_t = np.concatenate(
        [np.where(diff >= 0, lam[h] ** np.maximum(diff, 0.0), 0.0)
         for h in range(H)], axis=1)
    lam_c = lam ** 128.0
    return (np.ascontiguousarray(dq_all.astype(BF)),
            np.ascontiguousarray(dk_all.astype(BF)),
            np.ascontiguousarray(dm_t.astype(BF)),
            [float(v) for v in lam_c])


def _tile_w(w):
    M, K = w.shape
    nm, nk = M // 128, K // 128
    t = w.reshape(nm, 128, nk, 128)
    t = np.ascontiguousarray(t.transpose(0, 3, 2, 1))
    return t.reshape(nm, 128, K)


def _tile_wd(w_down):
    t = w_down.reshape(16, 128, IC, 128)
    return np.ascontiguousarray(t.transpose(0, 3, 2, 1))


def _tile_x(xT_half):
    t = xT_half.reshape(16, 128, T).transpose(1, 0, 2)
    return np.ascontiguousarray(t)


class _Sem:
    def __init__(self, nc, name):
        self.h = nc.alloc_semaphore(name)
        self.n = 0

    def inc(self, inst, k=1):
        inst.then_inc(self.h, k)
        self.n += k
        return self.n


def build_program(taps=(), use_cc=True):
    taps = set(taps)
    nc = bass.Bass("TRN2", target_bir_lowering=False)

    x_own = nc.dram_tensor("x_own", [128, 16, T], F32, kind="ExternalInput").ap()
    wqkv_t = nc.dram_tensor("wqkv_t", [48, 128, 2048], F32, kind="ExternalInput").ap()
    wo_t = nc.dram_tensor("wo_t", [16, 128, 2048], F32, kind="ExternalInput").ap()
    wg_t = nc.dram_tensor("wg_t", [IC, 128, 2048], F32, kind="ExternalInput").ap()
    wu_t = nc.dram_tensor("wu_t", [IC, 128, 2048], F32, kind="ExternalInput").ap()
    wd_t = nc.dram_tensor("wd_t", [16, 128, IC, 128], F32, kind="ExternalInput").ap()
    b_til = nc.dram_tensor("b_til", [128, 48], F32, kind="ExternalInput").ap()
    bo_til = nc.dram_tensor("bo_til", [128, 16], F32, kind="ExternalInput").ap()
    gs_til = nc.dram_tensor("gs_til", [128, IC], F32, kind="ExternalInput").ap()
    us_til = nc.dram_tensor("us_til", [128, IC], F32, kind="ExternalInput").ap()
    ds_til = nc.dram_tensor("ds_til", [128, 16], F32, kind="ExternalInput").ap()
    sflag = nc.dram_tensor("sflag", [128, 1], F32, kind="ExternalInput").ap()
    cflag = nc.dram_tensor("cflag", [128, 1], F32, kind="ExternalInput").ap()
    out_t = nc.dram_tensor("out_t", [128, 16, T], F32, kind="ExternalOutput").ap()
    cc_in = nc.dram_tensor("cc_in", [128, 2048], BF16, kind="Internal").ap()
    cc_out = nc.dram_tensor("cc_out", [128, 2048], BF16, kind="Internal").ap()

    dq_np, dk_np, dmt_np, lam_c = _consts()
    ident_d = nc.inline_tensor(np.eye(128, dtype=BF), "identc").ap()
    onescol_d = nc.inline_tensor(np.ones((128, 1), dtype=BF), "onescolc").ap()
    onesrow_d = nc.inline_tensor(np.ones((1, 128), dtype=np.float32), "onesrowc").ap()
    lamI_np = np.zeros((128, H * 128), dtype=np.float64)
    for h in range(H):
        lamI_np[:, h * 128:(h + 1) * 128][np.arange(128), np.arange(128)] = lam_c[h]
    lamI_d = nc.inline_tensor(np.ascontiguousarray(lamI_np.astype(BF)), "lamic").ap()
    lamc_np = np.repeat(np.array(lam_c, dtype=np.float64)[None, :], 128, 0)
    lamc_np = np.repeat(lamc_np, 128, 1).astype(BF)
    lamc_d = nc.inline_tensor(np.ascontiguousarray(lamc_np), "lamcc").ap()
    dq_d = nc.inline_tensor(dq_np, "dqc").ap()
    dk_d = nc.inline_tensor(dk_np, "dkc").ap()
    dmt_d = nc.inline_tensor(dmt_np, "dmtc").ap()

    tap_d = {}
    for name, shape in (
        ("kf_ctx", [128, 16, T]), ("vf_ctx", [128, 16, T]), ("S", [128, 2048]),
        ("qf", [128, 16, T]), ("kf", [128, 16, T]), ("vf", [128, 16, T]),
        ("attn", [128, 16, T]), ("r", [128, 16, T]), ("x2", [128, 16, T]),
        ("hdn1", [128, ICH, T]),
    ):
        if name in taps:
            tap_d[name] = nc.dram_tensor("tap_" + name, shape, F32,
                                         kind="ExternalOutput").ap()

    ident = nc.alloc_sbuf_tensor("ident_sb", [128, 128], BF16).ap()
    ones_col = nc.alloc_sbuf_tensor("onescol_sb", [128, 1], BF16).ap()
    ones_row = nc.alloc_sbuf_tensor("onesrow_sb", [1, 128], F32).ap()
    dq_all = nc.alloc_sbuf_tensor("dq_sb", [128, 2048], BF16).ap()
    dk_all = nc.alloc_sbuf_tensor("dk_sb", [128, 2048], BF16).ap()
    dmt_all = nc.alloc_sbuf_tensor("dmt_sb", [128, 2048], BF16).ap()
    lamI_sb = nc.alloc_sbuf_tensor("lamI_sb", [128, 2048], BF16).ap()
    b_sb = nc.alloc_sbuf_tensor("b_sb", [128, 48], F32).ap()
    bo_sb = nc.alloc_sbuf_tensor("bo_sb", [128, 16], F32).ap()
    gs_sb = nc.alloc_sbuf_tensor("gs_sb", [128, IC], F32).ap()
    us_sb = nc.alloc_sbuf_tensor("us_sb", [128, IC], F32).ap()
    ds_sb = nc.alloc_sbuf_tensor("ds_sb", [128, 16], F32).ap()
    flag_sb = nc.alloc_sbuf_tensor("flag_sb", [128, 1], F32).ap()
    cflag_sb = nc.alloc_sbuf_tensor("cflag_sb", [128, 1], F32).ap()
    s1_row = nc.alloc_sbuf_tensor("s1_row", [1, T], F32).ap()
    s2_row = nc.alloc_sbuf_tensor("s2_row", [1, T], F32).ap()
    s3_row = nc.alloc_sbuf_tensor("s3_row", [1, T], F32).ap()

    KB = 1024
    dyn = ((229344 - nc.sbuf_bytes_remaining) + 255) // 256 * 256
    assert dyn + 175 * KB <= 229344, dyn

    def at(name, shape, dtype, off):
        return nc.alloc_sbuf_tensor_at(name, shape, dtype, offset=dyn + off).ap()

    xb = at("xb", [128, 16, T], BF16, 0)
    attn_f = at("attn_f", [128, 16, T], BF16, 0)
    hdn_a = at("hdn_a", [128, 16, T], BF16, 0)
    kf = at("kf", [128, 16, T], BF16, 32 * KB)
    vf = at("vf", [128, 16, T], BF16, 64 * KB)
    r_f32 = at("r_f32", [128, 16, T], F32, 32 * KB)
    cstage = at("cstage", [128, 2048], BF16, 32 * KB)
    recv_sb = at("recv_sb", [128, 2048], BF16, 64 * KB)
    qf = at("qf", [128, 16, T], BF16, 96 * KB)
    x2b = at("x2b", [128, 16, T], BF16, 96 * KB)
    WO = 128 * KB
    wpA = at("wpA", [128, 3, 2048], BF16, WO + 0)
    outbuf = at("outbuf", [128, 8, T], F32, 96 * KB)
    wpB = at("wpB", [128, 2, 2048], BF16, WO + 8 * KB)
    xres = at("xres", [128, 2, T], F32, WO + 8 * KB)
    wpD = at("wpD", [128, 2, ICH * 128], BF16, WO + 16 * KB)
    ntmp = at("ntmp", [128, 2, T], F32, WO + 16 * KB)
    XO = WO + 27 * KB
    ks2 = at("ks2", [128, 2, 2048], BF16, WO + 0)
    dql = at("dql", [128, 2048], BF16, WO + 8 * KB)
    lamc_sb = at("lamc_sb", [128, 2048], BF16, XO + 0)
    vtm2 = at("vtm2", [128, 2, 2048], BF16, WO + 8 * KB)
    sm_sb = at("sm_sb", [128, 2048], BF16, XO + 8 * KB)
    qs_sb = at("qs_sb", [128, 2048], BF16, XO + 12 * KB)
    S_bf = at("S_bf", [128, 2048], BF16, XO + 16 * KB)
    hdn_b = at("hdn_b", [128, 6, T], BF16, XO + 0)
    rb_buf = at("rb_buf", [128, 2, T], BF16, XO + 12 * KB)
    sq_buf = at("sq_buf", [128, 2, T], BF16, XO + 16 * KB)
    silu_buf = at("silu_buf", [128, 2, T], BF16, XO + 12 * KB)
    mbc_sb = at("mbc_sb", [128, 2, T], F32, XO + 12 * KB)

    def hdn_tile(i):
        sl = i % ICH
        return hdn_a[:, sl, :] if sl < 16 else hdn_b[:, sl - 16, :]

    s_in = _Sem(nc, "s_in")
    s_w = _Sem(nc, "s_w")
    s_pe = _Sem(nc, "s_pe")
    s_ac = _Sem(nc, "s_ac")
    s_ve = _Sem(nc, "s_ve")
    s_st = _Sem(nc, "s_st")
    s_gp = _Sem(nc, "s_gp")

    stA = ExitStack()
    stT = ExitStack()
    psA = stA.enter_context(nc.psum_tensor("psA", [128, 2048], F32)).ap()
    psT = stT.enter_context(nc.psum_tensor("psT", [128, 2048], BF16)).ap()
    psT2 = stT.enter_context(nc.psum_tensor("psT2", [128, 2048], BF16)).ap()

    mk = {}

    def W(eng, *pairs):
        for s, v in pairs:
            if v:
                eng.wait_ge(s.h, v)

    with nc.Block() as blk:
        @blk.sync
        def _(sy):
            for dst, src in ((ident, ident_d), (ones_row, onesrow_d),
                             (dq_all, dq_d), (dk_all, dk_d), (dmt_all, dmt_d),
                             (b_sb, b_til), (bo_sb, bo_til), (gs_sb, gs_til),
                             (us_sb, us_til), (ds_sb, ds_til), (flag_sb, sflag),
                             (cflag_sb, cflag), (ones_col, onescol_d),
                             (lamI_sb, lamI_d), (lamc_sb, lamc_d)):
                s_in.inc(sy.dma_start(out=dst[:, :], in_=src[:, :]), 16)
        consts_done = s_in.n

        @blk.gpsimd
        def _(g):
            s_w.inc(g.dma_start(out=xb[:, :, :], in_=x_own[:, :, :]), 16)
        xb_w = s_w.n

        @blk.vector
        def _(v):
            W(v, (s_in, consts_done))
            mk["s_zero"] = s_ve.inc(v.memset(S_bf[:, :], 0.0))


        def proj_pass(n_pieces, piece_src, rhs3, dest_of, bias_of,
                      pe_first=(), gp_first=()):
            mm_at, ev_at = {}, {}
            for mi in range(n_pieces):
                @blk.gpsimd
                def _(g, mi=mi):
                    if mi == 0:
                        W(g, *gp_first)
                    if mi >= 3:
                        W(g, (s_pe, mm_at[mi - 3]))
                    s_w.inc(g.dma_start(out=wpA[:, mi % 3, :],
                                        in_=piece_src(mi)), 16)
                w_done = s_w.n

                @blk.tensor
                def _(te, mi=mi, w_done=w_done):
                    W(te, (s_w, w_done))
                    if mi == 0:
                        W(te, (s_in, consts_done), *pe_first)
                    if mi >= 2:
                        W(te, (s_ac, ev_at[mi - 2]))
                    half = mi % 2
                    for kc2 in range(KC):
                        w_ap = wpA[:, mi % 3, kc2 * 128:(kc2 + 1) * 128]
                        te.matmul(psA[:, half * 1024:half * 1024 + 512],
                                  w_ap, rhs3[:, kc2, 0:512],
                                  start=(kc2 == 0), stop=False)
                        mm = te.matmul(psA[:, half * 1024 + 512:(half + 1) * 1024],
                                       w_ap, rhs3[:, kc2, 512:1024],
                                       start=(kc2 == 0), stop=(kc2 == KC - 1))
                    mm_at[mi] = s_pe.inc(mm)

                @blk.scalar
                def _(s, mi=mi):
                    W(s, (s_pe, mm_at[mi]))
                    half = mi % 2
                    ins = s.activation(dest_of(mi),
                                       psA[:, half * 1024:(half + 1) * 1024],
                                       AF.Identity, bias=bias_of(mi), scale=1.0)
                    ev_at[mi] = s_ac.inc(ins)
            return mm_at, ev_at

        def scanA1(c):
            cs = slice(c * 128, (c + 1) * 128)

            @blk.tensor
            def _(te):
                W(te, mk.get("psT_free", (s_ve, 0)), mk.pop("kvf_ready", (s_ac, 0)))
                for h in range(H):
                    t = te.transpose(psT[:, h * 128:(h + 1) * 128], kf[:, h, cs],
                                     ident[:, :])
                mk["tk"] = s_pe.inc(t)
                W(te, mk.get("psT2_free", (s_ac, 0)))
                for h in range(H):
                    t = te.transpose(psT2[:, h * 128:(h + 1) * 128], vf[:, h, cs],
                                     ident[:, :])
                mk["tv"] = s_pe.inc(t)

        def scanA2(c):
            cs = slice(c * 128, (c + 1) * 128)
            p = c % 2
            slot_free = mk.get("ut1_c%d" % (c - 2), 0)

            @blk.tensor
            def _(te):
                for gg in range(2):
                    W(te, mk.get("psA%d_free" % gg, (s_ve, 0)))
                    for h in range(gg * 8, gg * 8 + 8):
                        t = te.matmul(psA[:, h * 128:(h + 1) * 128], kf[:, h, cs],
                                      qf[:, h, cs], start=(h % 4 == 0),
                                      stop=(h % 4 == 3), skip_group_check=True)
                    mk["sc%d" % gg] = s_pe.inc(t)

            @blk.vector
            def _(v, c=c, p=p, slot_free=slot_free):
                W(v, mk.pop("qf_ready", (s_ac, 0)))
                W(v, (s_pe, mk["tk"]), (s_pe, slot_free))
                mk["ks"] = s_ve.inc(
                    v.tensor_tensor(ks2[:, p, :], psT[:, :], dk_all[:, :], ALU.mult))
                v.tensor_tensor(qs_sb.rearrange("p (h i) -> p h i", h=H),
                                qf[:, :, cs],
                                dq_all.rearrange("p (h i) -> p h i", h=H),
                                ALU.mult)
                for gg in range(2):
                    gs_ = slice(gg * 1024, (gg + 1) * 1024)
                    W(v, (s_pe, mk["sc%d" % gg]))
                    mk["sm%d" % gg] = s_ve.inc(
                        v.tensor_tensor(sm_sb[:, gs_], psA[:, gs_],
                                        dmt_all[:, gs_], ALU.mult))
            mk["psT_free"] = (s_ve, mk["ks"])

            @blk.scalar
            def _(s, c=c, p=p, slot_free=slot_free):
                W(s, (s_pe, mk["tv"]), (s_pe, slot_free))
                mk["vtm"] = s_ac.inc(s.copy(vtm2[:, p, :], psT2[:, :]))
            mk["psT2_free"] = (s_ac, mk["vtm"])



        def scanB_at(c):
            cs = slice(c * 128, (c + 1) * 128)
            p = c % 2
            for gg in range(2):
                @blk.tensor
                def _(te, gg=gg):
                    W(te, (s_ve, mk["sm%d" % gg]))
                    if gg == 0:
                        W(te, (s_ac, mk["vtm"]))
                    for h in range(gg * 8, gg * 8 + 8):
                        hs = slice(h * 128, (h + 1) * 128)
                        te.matmul(psA[:, hs], S_bf[:, hs], qs_sb[:, hs],
                                  start=(h % 4 == 0), stop=False,
                                  skip_group_check=True)
                        t = te.matmul(psA[:, hs], vtm2[:, p, hs], sm_sb[:, hs],
                                      start=False, stop=(h % 4 == 3),
                                      skip_group_check=True)
                    mk["at%d" % gg] = s_pe.inc(t)

                @blk.scalar
                def _(s, gg=gg):
                    W(s, (s_pe, mk["at%d" % gg]))
                    mk["atev%d" % gg] = s_ac.inc(
                        s.copy(attn_f[:, gg * 8:(gg + 1) * 8, cs],
                               psA[:, gg * 1024:(gg + 1) * 1024]
                               .rearrange("p (h i) -> p h i", h=8)))

        def scanB_ut(c):
            p = c % 2
            for gg in range(2):
                @blk.tensor
                def _(te, gg=gg):
                    W(te, (s_ac, mk["atev%d" % gg]))
                    for h in range(gg * 8, gg * 8 + 8):
                        hs = slice(h * 128, (h + 1) * 128)
                        te.matmul(psA[:, hs], ks2[:, p, hs], vtm2[:, p, hs],
                                  start=(h % 4 == 0), stop=False,
                                  skip_group_check=True)
                        t = te.matmul(psA[:, hs], lamI_sb[:, hs], S_bf[:, hs],
                                      start=False, stop=(h % 4 == 3),
                                      skip_group_check=True)
                    mk["ut%d" % gg] = s_pe.inc(t)
            mk["ut1_c%d" % c] = mk["ut1"]

            for gg in range(2):
                @blk.scalar
                def _(s, gg=gg, c=c):
                    W(s, (s_pe, mk["ut%d" % gg]))
                    gsl = slice(gg * 1024, (gg + 1) * 1024)
                    mk["supd%d" % gg] = s_ac.inc(s.copy(S_bf[:, gsl], psA[:, gsl]))
                mk["psA%d_free" % gg] = (s_ac, mk["supd%d" % gg])

        def own_dest(mi):
            return (qf, kf, vf)[mi // 16][:, mi % 16, :]
        mm3, ev3 = proj_pass(
            48, lambda mi: wqkv_t[mi, :, :], xb,
            own_dest, lambda mi: b_sb[:, mi:mi + 1])
        seg3_end_pe = s_pe.n

        for tname, tsrc in (("qf", qf), ("kf", kf), ("vf", vf)):
            if tname in taps:
                @blk.gpsimd
                def _(g, tname=tname, tsrc=tsrc):
                    W(g, (s_ac, ev3[47]))
                    s_st.inc(g.dma_start(out=tap_d[tname][:, :, :],
                                         in_=tsrc[:, :, :]), 16)

        mk["psT_free"] = (s_ve, 0)
        mk["psA0_free"] = (s_ac, ev3[47])
        mk["psA1_free"] = (s_ac, ev3[47])
        mk["kvf_ready"] = (s_ac, ev3[47])
        mk["qf_ready"] = (s_ac, ev3[47])
        scanA1(0)
        scanA2(0)
        for c in range(NCH):
            scanB_at(c)
            if c + 1 < NCH:
                scanA1(c + 1)
            scanB_ut(c)
            if c + 1 < NCH:
                scanA2(c + 1)
        scan_end_pe = mk["ut1"]
        attn_all = mk["supd1"]

        s_cc = _Sem(nc, "s_cc")

        @blk.vector
        def _(v):
            W(v, (s_ac, mk["supd1"]))
            mk["cst"] = s_ve.inc(
                v.tensor_scalar_mul(cstage[:, :], S_bf[:, :], cflag_sb[:, 0:1]))

        @blk.sync
        def _(sy):
            W(sy, (s_ve, mk["cst"]))
            mk["cc_st"] = s_st.inc(sy.dma_start(out=cc_in[:, :], in_=cstage[:, :]), 16)

        @blk.gpsimd
        def _(g):
            W(g, (s_st, mk["cc_st"]))
            if use_cc:
                s_cc.inc(g.collective_compute(
                    "AllReduce", ALU.add,
                    replica_groups=[[0, 1], [2, 3], [4, 5], [6, 7]],
                    ins=[cc_in.opt()], outs=[cc_out.opt()]))
            else:
                s_cc.inc(g.dma_start(out=cc_out[:, :], in_=cc_in[:, :]), 16)

        @blk.sync
        def _(sy):
            W(sy, (s_cc, s_cc.n))
            mk["rc_ld"] = s_in.inc(sy.dma_start(out=recv_sb[:, :], in_=cc_out[:, :]), 16)

        @blk.vector
        def _(v):
            W(v, (s_in, mk["rc_ld"]))
            mk["sprev"] = s_ve.inc(
                v.tensor_scalar_mul(S_bf[:, :], recv_sb[:, :], flag_sb[:, 0:1]))

        for l in range(NCH):
            ls = slice(l * 128, (l + 1) * 128)

            @blk.vector
            def _(v, l=l):
                if l == 0:
                    v.tensor_copy(dql[:, :], dq_all[:, :])
                else:
                    v.tensor_tensor(dql[:, :], dql[:, :], lamc_sb[:, :], ALU.mult)
                mk["cqs"] = s_ve.inc(
                    v.tensor_tensor(qs_sb.rearrange("p (h i) -> p h i", h=H),
                                    qf[:, :, ls],
                                    dql.rearrange("p (h i) -> p h i", h=H),
                                    ALU.mult))

            for gg in range(2):
                @blk.tensor
                def _(te, l=l, gg=gg):
                    if gg == 0:
                        W(te, (s_ve, mk["cqs"]))
                        if l == 0:
                            W(te, (s_ve, mk["sprev"]))
                    for h in range(gg * 8, gg * 8 + 8):
                        hs = slice(h * 128, (h + 1) * 128)
                        t = te.matmul(psA[:, hs], S_bf[:, hs], qs_sb[:, hs],
                                      start=(h % 4 == 0), stop=(h % 4 == 3),
                                      skip_group_check=True)
                    mk["cmm%d" % gg] = s_pe.inc(t)

                @blk.vector
                def _(v, l=l, gg=gg):
                    W(v, (s_pe, mk["cmm%d" % gg]))
                    gsl = slice(gg * 1024, (gg + 1) * 1024)
                    t = v.scalar_tensor_tensor(
                        attn_f[:, gg * 8:(gg + 1) * 8, ls],
                        psA[:, gsl].rearrange("p (h i) -> p h i", h=8), 1.0,
                        attn_f[:, gg * 8:(gg + 1) * 8, ls],
                        ALU.mult, ALU.add)
                    mk["corr%d" % gg] = s_ve.inc(t)
        attn_all = mk["corr1"]

        if "attn" in taps:
            @blk.gpsimd
            def _(g):
                W(g, (s_ve, attn_all))
                s_st.inc(g.dma_start(out=tap_d["attn"][:, :, :], in_=attn_f[:, :, :]), 16)

        stT.close()
        stS = ExitStack()
        psS1 = stS.enter_context(nc.psum_tensor("psS1", [1, T], F32)).ap()
        psS2 = stS.enter_context(nc.psum_tensor("psS2", [1, T], F32)).ap()

        rb_at, sq_at, st1_at = {}, {}, {}
        r_at, omm_at = {}, {}

        def stats_mm(j, ps1, ps2):
            @blk.tensor
            def _(te, j=j):
                W(te, (s_ve, rb_at[j]))
                te.matmul(ps1[0:1, 0:512], ones_col[:, :], rb_buf[:, j % 2, 0:512],
                          start=(j == 0), stop=(j == 15), skip_group_check=True)
                te.matmul(ps1[0:1, 512:1024], ones_col[:, :], rb_buf[:, j % 2, 512:1024],
                          start=(j == 0), stop=(j == 15), skip_group_check=True)
                W(te, (s_ac, sq_at[j]))
                te.matmul(ps2[0:1, 0:512], ones_col[:, :], sq_buf[:, j % 2, 0:512],
                          start=(j == 0), stop=(j == 15), skip_group_check=True)
                t = te.matmul(ps2[0:1, 512:1024], ones_col[:, :],
                              sq_buf[:, j % 2, 512:1024],
                              start=(j == 0), stop=(j == 15), skip_group_check=True)
                st1_at[j] = s_pe.inc(t)

        def ln_rows_and_bcast(ps1, ps2, psBC):
            inv = 1.0 / DM
            @blk.vector
            def _(v):
                W(v, (s_pe, st1_at[15]))
                v.tensor_scalar_mul(s1_row[0:1, :], ps1[0:1, :], inv)
                v.tensor_scalar_mul(s2_row[0:1, :], ps2[0:1, :], inv)
                v.tensor_tensor(s3_row[0:1, :], s1_row[0:1, :], s1_row[0:1, :], ALU.mult)
                v.tensor_tensor(s2_row[0:1, :], s2_row[0:1, :], s3_row[0:1, :],
                                ALU.subtract)
                t = v.tensor_scalar_add(s2_row[0:1, :], s2_row[0:1, :], EPS)
                mk["var"] = s_ve.inc(t)

            @blk.scalar
            def _(s):
                W(s, (s_ve, mk["var"]))
                s.activation(s3_row[0:1, :], s2_row[0:1, :], AF.Ln)
                mk["rstd"] = s_ac.inc(
                    s.activation(s3_row[0:1, :], s3_row[0:1, :], AF.Exp, scale=-0.5))

            @blk.tensor
            def _(te):
                W(te, (s_ac, mk["rstd"]))
                for hh in range(2):
                    te.matmul(psBC[:, hh * 512:hh * 512 + 512], ones_row[0:1, :],
                              s1_row[0:1, hh * 512:hh * 512 + 512],
                              start=True, stop=True, skip_group_check=True)
                    t = te.matmul(psBC[:, 1024 + hh * 512:1536 + hh * 512],
                                  ones_row[0:1, :],
                                  s3_row[0:1, hh * 512:hh * 512 + 512],
                                  start=True, stop=True, skip_group_check=True)
                mk["bc"] = s_pe.inc(t)

            @blk.vector
            def _(v):
                W(v, (s_pe, mk["bc"]))
                v.tensor_copy(mbc_sb[:, 0, :], psBC[:, 0:1024])
                t = v.tensor_copy(mbc_sb[:, 1, :], psBC[:, 1024:2048])
                mk["mbc"] = s_ve.inc(t)

        for j in range(16):
            @blk.gpsimd
            def _(g, j=j):
                if j <= 1:
                    W(g, (s_pe, scan_end_pe))
                if j >= 2:
                    W(g, (s_pe, omm_at[j - 2]))
                s_w.inc(g.dma_start(out=wpA[:, j % 2, :], in_=wo_t[j, :, :]), 16)
            wj = s_w.n

            @blk.sync
            def _(sy, j=j):
                if j <= 1:
                    W(sy, (s_ve, attn_all))
                if j >= 2:
                    W(sy, (s_ve, r_at[j - 2]))
                s_in.inc(sy.dma_start(out=xres[:, j % 2, :], in_=x_own[:, j, :]), 16)
            xj = s_in.n

            @blk.tensor
            def _(te, j=j, wj=wj):
                W(te, (s_w, wj))
                if j == 0:
                    W(te, (s_ve, attn_all))
                if j >= 2:
                    W(te, (s_ve, r_at[j - 2]))
                half = j % 2
                for kc2 in range(KC):
                    w_ap = wpA[:, half, kc2 * 128:(kc2 + 1) * 128]
                    te.matmul(psA[:, half * 1024:half * 1024 + 512], w_ap,
                              attn_f[:, kc2, 0:512], start=(kc2 == 0), stop=False)
                    t = te.matmul(psA[:, half * 1024 + 512:(half + 1) * 1024], w_ap,
                                  attn_f[:, kc2, 512:1024], start=(kc2 == 0),
                                  stop=(kc2 == KC - 1))
                omm_at[j] = s_pe.inc(t)

            @blk.vector
            def _(v, j=j, xj=xj):
                W(v, (s_pe, omm_at[j]), (s_in, xj))
                half = j % 2
                v.scalar_tensor_tensor(r_f32[:, j, :],
                                       psA[:, half * 1024:(half + 1) * 1024],
                                       bo_sb[:, j:j + 1], xres[:, half, :],
                                       ALU.add, ALU.add)
                t = v.tensor_copy(rb_buf[:, j % 2, :], r_f32[:, j, :])
                r_at[j] = rb_at[j] = s_ve.inc(t)

            @blk.scalar
            def _(s, j=j):
                W(s, (s_ve, rb_at[j]))
                sq_at[j] = s_ac.inc(
                    s.activation(sq_buf[:, j % 2, :], rb_buf[:, j % 2, :], AF.Square))
            if j >= 1:
                stats_mm(j - 1, psS1, psS2)
            if j == 15:
                stats_mm(15, psS1, psS2)

        if "r" in taps:
            @blk.sync
            def _(sy):
                W(sy, (s_ve, r_at[15]))
                s_st.inc(sy.dma_start(out=tap_d["r"][:, :, :], in_=r_f32[:, :, :]), 16)

        ln_rows_and_bcast(psS1, psS2, psA)
        bc1 = mk["bc"]

        pre_w = {}

        @blk.gpsimd
        def _(g):
            W(g, (s_pe, omm_at[15]), (s_ve, r_at[15]))
            for i in (0, 1):
                s_w.inc(g.dma_start(out=wpA[:, i % 2, :], in_=wg_t[i, :, :]), 16)
                s_w.inc(g.dma_start(out=wpB[:, i % 2, :], in_=wu_t[i, :, :]), 16)
                pre_w[i] = s_w.n

        x2b_rdy = {}
        for j in range(13):
            @blk.vector
            def _(v, j=j):
                if j == 0:
                    W(v, (s_ve, mk["mbc"]))
                v.tensor_tensor(ntmp[:, 0, :], r_f32[:, j, :],
                                mbc_sb[:, 0, :], ALU.subtract)
                x2b_rdy[j] = ("v", s_ve.inc(
                    v.tensor_tensor(x2b[:, j, :], ntmp[:, 0, :],
                                    mbc_sb[:, 1, :], ALU.mult)))
        for j in range(13, 16):
            @blk.gpsimd
            def _(g, j=j):
                if j == 13:
                    g.wait_ge(s_ve.h, mk["mbc"])
                g.tensor_tensor(ntmp[:, 1, :], r_f32[:, j, :],
                                mbc_sb[:, 0, :], ALU.subtract)
                x2b_rdy[j] = ("g", s_gp.inc(
                    g.tensor_tensor(x2b[:, j, :], ntmp[:, 1, :],
                                    mbc_sb[:, 1, :], ALU.mult)))
        ln1_doneV = x2b_rdy[12][1]
        ln1_doneP = x2b_rdy[15][1]
        ln1_done = ln1_doneV

        if "x2" in taps:
            @blk.gpsimd
            def _(g):
                g.wait_ge(s_ve.h, ln1_doneV)
                g.wait_ge(s_gp.h, ln1_doneP)
                s_st.inc(g.dma_start(out=tap_d["x2"][:, :, :], in_=x2b[:, :, :]), 16)

        stS.close()
        stA.close()

        gu_mm_at, hdn_at = {}, {}
        dmm_at, r2_at = {}, {}

        def gate_up(psG, i_lo, i_hi, gp_first=(), pe_first=(), preload=None):
            for i in range(i_lo, i_hi):
                io = i - i_lo

                if preload is not None and io < 2:
                    wi = preload[io]
                else:
                    @blk.gpsimd
                    def _(g, i=i, io=io):
                        if io == 0:
                            W(g, *gp_first)
                        if io >= 2:
                            W(g, (s_pe, gu_mm_at[i - 2]))
                        s_w.inc(g.dma_start(out=wpA[:, i % 2, :], in_=wg_t[i, :, :]), 16)
                        s_w.inc(g.dma_start(out=wpB[:, i % 2, :], in_=wu_t[i, :, :]), 16)
                    wi = s_w.n

                @blk.tensor
                def _(te, i=i, io=io, wi=wi):
                    W(te, (s_w, wi))
                    if io == 0:
                        W(te, *pe_first)
                    if io >= 2:
                        W(te, (s_ve, hdn_at[i - 2]))
                    q = (i % 2) * 2048
                    for kc2 in range(KC):
                        if io == 0 and i == 0 and kc2 in x2b_rdy:
                            eng, vv = x2b_rdy[kc2]
                            W(te, (s_ve if eng == "v" else s_gp, vv))
                        wg_ap = wpA[:, i % 2, kc2 * 128:(kc2 + 1) * 128]
                        wu_ap = wpB[:, i % 2, kc2 * 128:(kc2 + 1) * 128]
                        te.matmul(psG[:, q:q + 512], wg_ap, x2b[:, kc2, 0:512],
                                  start=(kc2 == 0), stop=False)
                        te.matmul(psG[:, q + 512:q + 1024], wg_ap, x2b[:, kc2, 512:1024],
                                  start=(kc2 == 0), stop=(kc2 == KC - 1))
                        te.matmul(psG[:, q + 1024:q + 1536], wu_ap, x2b[:, kc2, 0:512],
                                  start=(kc2 == 0), stop=False)
                        t = te.matmul(psG[:, q + 1536:q + 2048], wu_ap,
                                      x2b[:, kc2, 512:1024],
                                      start=(kc2 == 0), stop=(kc2 == KC - 1))
                    gu_mm_at[i] = s_pe.inc(t)

                @blk.scalar
                def _(s, i=i):
                    W(s, (s_pe, gu_mm_at[i]))
                    q = (i % 2) * 2048
                    mk["sil"] = s_ac.inc(
                        s.activation(silu_buf[:, i % 2, :], psG[:, q:q + 1024],
                                     AF.Silu, scale=gs_sb[:, i:i + 1]))

                @blk.vector
                def _(v, i=i):
                    W(v, (s_ac, mk["sil"]))
                    q = (i % 2) * 2048
                    hdn_at[i] = s_ve.inc(
                        v.scalar_tensor_tensor(hdn_tile(i), psG[:, q + 1024:q + 2048],
                                               us_sb[:, i:i + 1], silu_buf[:, i % 2, :],
                                               ALU.mult, ALU.mult))

        def down_pass(psD, ih, ps1=None, ps2=None, gp_first=()):
            for j in range(16):
                @blk.gpsimd
                def _(g, j=j):
                    if j == 0:
                        W(g, *gp_first)
                    if j >= 2:
                        W(g, (s_pe, dmm_at[(ih, j - 2)]))
                    s_w.inc(g.dma_start(
                        out=wpD[:, j % 2, :],
                        in_=wd_t[j, :, ih * ICH:(ih + 1) * ICH, :]), 16)
                wj = s_w.n

                @blk.tensor
                def _(te, j=j, wj=wj):
                    W(te, (s_w, wj))
                    if j == 0:
                        W(te, (s_ve, hdn_at[ih * ICH + ICH - 1]))
                    if j >= 2:
                        W(te, (s_ve, r2_at[(ih, j - 2)]))
                    half = j % 2
                    for ii in range(ICH):
                        w_ap = wpD[:, half, ii * 128:(ii + 1) * 128]
                        ht = hdn_tile(ih * ICH + ii)
                        te.matmul(psD[:, half * 1024:half * 1024 + 512], w_ap,
                                  ht[:, 0:512], start=(ii == 0), stop=False)
                        t = te.matmul(psD[:, half * 1024 + 512:(half + 1) * 1024], w_ap,
                                      ht[:, 512:1024], start=(ii == 0),
                                      stop=(ii == ICH - 1))
                    dmm_at[(ih, j)] = s_pe.inc(t)

                @blk.vector
                def _(v, j=j):
                    W(v, (s_pe, dmm_at[(ih, j)]))
                    half = j % 2
                    if ih == 0:
                        t = v.scalar_tensor_tensor(
                            r_f32[:, j, :], psD[:, half * 1024:(half + 1) * 1024],
                            ds_sb[:, j:j + 1], x2b[:, j, :], ALU.mult, ALU.add)
                    else:
                        v.scalar_tensor_tensor(
                            r_f32[:, j, :], psD[:, half * 1024:(half + 1) * 1024],
                            ds_sb[:, j:j + 1], r_f32[:, j, :], ALU.mult, ALU.add)
                        t = v.tensor_copy(rb_buf[:, j % 2, :], r_f32[:, j, :])
                    r2_at[(ih, j)] = s_ve.inc(t)

                if ih == 1:
                    rb_at[j] = r2_at[(1, j)]

                    @blk.scalar
                    def _(s, j=j):
                        W(s, (s_ve, rb_at[j]))
                        sq_at[j] = s_ac.inc(
                            s.activation(sq_buf[:, j % 2, :], rb_buf[:, j % 2, :],
                                         AF.Square))
                    if j >= 1:
                        stats_mm(j - 1, ps1, ps2)
                    if j == 15:
                        stats_mm(15, ps1, ps2)

        st6 = ExitStack()
        psG = st6.enter_context(nc.psum_tensor("psG", [128, 4096], F32)).ap()
        gate_up(psG, 0, ICH, preload=pre_w)
        if "hdn1" in taps:
            @blk.gpsimd
            def _(g):
                W(g, (s_ve, hdn_at[ICH - 1]))
                for i in range(ICH):
                    s_st.inc(g.dma_start(out=tap_d["hdn1"][:, i, :], in_=hdn_tile(i)), 16)

        st6.close()
        st7 = ExitStack()
        psD = st7.enter_context(nc.psum_tensor("psD", [128, 2048], F32)).ap()
        down_pass(psD, 0, gp_first=((s_ve, ln1_doneV), (s_gp, ln1_doneP)))
        seg7_r2 = r2_at[(0, 15)]

        st7.close()
        st8 = ExitStack()
        psG2 = st8.enter_context(nc.psum_tensor("psG2", [128, 4096], F32)).ap()
        gate_up(psG2, ICH, IC,
                gp_first=((s_pe, gu_mm_at[ICH - 1]),),
                pe_first=((s_ve, seg7_r2),))

        st8.close()
        st9 = ExitStack()
        psD2 = st9.enter_context(nc.psum_tensor("psD2", [128, 2048], F32)).ap()
        psS1b = st9.enter_context(nc.psum_tensor("psS1b", [1, T], F32)).ap()
        psS2b = st9.enter_context(nc.psum_tensor("psS2b", [1, T], F32)).ap()
        down_pass(psD2, 1, ps1=psS1b, ps2=psS2b,
                  gp_first=((s_pe, dmm_at[(0, 15)]),))

        ln_rows_and_bcast(psS1b, psS2b, psD2)
        bc2 = mk["bc"]

        out_st = {}
        onrm = {}
        for j in range(13):
            @blk.vector
            def _(v, j=j):
                if j == 0:
                    W(v, (s_ve, mk["mbc"]))
                if j >= 4:
                    W(v, (s_st, out_st[j - 4]))
                v.tensor_tensor(ntmp[:, 0, :], r_f32[:, j, :],
                                mbc_sb[:, 0, :], ALU.subtract)
                onrm[j] = ("v", s_ve.inc(
                    v.tensor_tensor(outbuf[:, j % 4, :], ntmp[:, 0, :],
                                    mbc_sb[:, 1, :], ALU.mult)))

            @blk.sync
            def _(sy, j=j):
                W(sy, (s_ve, onrm[j][1]))
                out_st[j] = s_st.inc(
                    sy.dma_start(out=out_t[:, j, :], in_=outbuf[:, j % 4, :]), 16)

        for j in range(13, 16):
            @blk.gpsimd
            def _(g, j=j):
                if j == 13:
                    g.wait_ge(s_ve.h, mk["mbc"])
                if j >= 15:
                    g.wait_ge(s_st.h, out_st[j - 4])
                g.tensor_tensor(ntmp[:, 1, :], r_f32[:, j, :],
                                mbc_sb[:, 0, :], ALU.subtract)
                onrm[j] = ("g", s_gp.inc(
                    g.tensor_tensor(outbuf[:, 4 + j % 4, :], ntmp[:, 1, :],
                                    mbc_sb[:, 1, :], ALU.mult)))

            @blk.sync
            def _(sy, j=j):
                W(sy, (s_gp, onrm[j][1]))
                out_st[j] = s_st.inc(
                    sy.dma_start(out=out_t[:, j, :], in_=outbuf[:, 4 + j % 4, :]), 16)

        @blk.sync
        def _(sy):
            W(sy, (s_st, s_st.n))
        st9.close()

    return nc


def _prepare(inputs):
    wqkv_t = _tile_w(np.asarray(inputs["L0_attn_w_qkv"], np.float32))
    wo_t = _tile_w(np.asarray(inputs["L0_attn_w_out"], np.float32))
    wg_t = _tile_w(np.asarray(inputs["L0_ffn_gate_w"], np.float32))
    wu_t = _tile_w(np.asarray(inputs["L0_ffn_up_w"], np.float32))
    wd_t = _tile_wd(np.asarray(inputs["L0_ffn_down_w"], np.float32))
    b_til = np.ascontiguousarray(
        np.asarray(inputs["L0_attn_b_qkv"], np.float32).reshape(48, 128).T)
    bo_til = np.ascontiguousarray(
        np.asarray(inputs["L0_attn_b_out"], np.float32).reshape(16, 128).T)
    gs_til = np.ascontiguousarray(
        np.asarray(inputs["L0_ffn_gate_s"], np.float32).reshape(IC, 128).T)
    us_til = np.ascontiguousarray(
        np.asarray(inputs["L0_ffn_up_s"], np.float32).reshape(IC, 128).T)
    ds_til = np.ascontiguousarray(
        np.asarray(inputs["L0_ffn_down_s"], np.float32).reshape(16, 128).T)
    return dict(wqkv_t=wqkv_t, wo_t=wo_t, wg_t=wg_t, wu_t=wu_t, wd_t=wd_t,
                b_til=b_til, bo_til=bo_til, gs_til=gs_til, us_til=us_til,
                ds_til=ds_til)


def make_in_maps(inputs):
    shared = _prepare(inputs)
    x = np.asarray(inputs["L0_attn_x"], np.float32)
    flag0 = np.zeros((128, 1), np.float32)
    flag1 = np.ones((128, 1), np.float32)
    in_maps = []
    for c in range(8):
        b, half = c // 2, c % 2
        xT = np.ascontiguousarray(x[b].T)
        m = dict(shared)
        m["x_own"] = _tile_x(xT[:, half * T:(half + 1) * T])
        m["sflag"] = flag1 if half == 1 else flag0
        m["cflag"] = flag0 if half == 1 else flag1
        in_maps.append(m)
    return in_maps


def kernel(**inputs):
    nc = build_program()
    in_maps = make_in_maps(inputs)
    res = run_bass_kernel_spmd(nc, in_maps, core_ids=list(range(8)))
    out = np.empty((BATCH, SEQ, DM), np.float32)
    for c in range(8):
        b, half = c // 2, c % 2
        ot = res.results[c]["out_t"]
        out[b, half * T:(half + 1) * T, :] = ot.transpose(1, 0, 2).reshape(DM, T).T
    return out


# version 18
# speedup vs baseline: 1.1014x; 1.0021x over previous
import math
from contextlib import ExitStack

import numpy as np
import ml_dtypes
import concourse.bass as bass
import concourse.mybir as mybir
from concourse.bass_utils import run_bass_kernel_spmd

F32 = mybir.dt.float32
BF16 = mybir.dt.bfloat16
AF = mybir.ActivationFunctionType
ALU = mybir.AluOpType

H = 16
DM = 2048
T = 1024
NCH = 8
KC = 16
INTER = 5632
IC = 44
ICH = 22
BATCH = 4
SEQ = 2048
EPS = 1e-5

BF = ml_dtypes.bfloat16


def _lam64():
    start = 2.0 ** (-2.0 ** (-(math.log2(H) - 3)))
    slopes = np.array([start ** (i + 1) for i in range(H)], dtype=np.float64)
    return np.exp(-slopes)


def _consts():
    lam = _lam64()
    i_idx = np.arange(128, dtype=np.float64)
    dq = (lam[:, None] ** (i_idx[None, :] + 1.0)).reshape(1, H * 128)
    dq_all = np.broadcast_to(dq, (128, H * 128))
    dk = lam[None, :] ** (127.0 - i_idx[:, None])
    dk_all = np.repeat(dk, 128, axis=1)
    diff = i_idx[None, :] - i_idx[:, None]
    dm_t = np.concatenate(
        [np.where(diff >= 0, lam[h] ** np.maximum(diff, 0.0), 0.0)
         for h in range(H)], axis=1)
    lam_c = lam ** 128.0
    return (np.ascontiguousarray(dq_all.astype(BF)),
            np.ascontiguousarray(dk_all.astype(BF)),
            np.ascontiguousarray(dm_t.astype(BF)),
            [float(v) for v in lam_c])


def _tile_w(w):
    M, K = w.shape
    nm, nk = M // 128, K // 128
    t = w.reshape(nm, 128, nk, 128)
    t = np.ascontiguousarray(t.transpose(0, 3, 2, 1))
    return t.reshape(nm, 128, K)


def _tile_wd(w_down):
    t = w_down.reshape(16, 128, IC, 128)
    return np.ascontiguousarray(t.transpose(0, 3, 2, 1))


def _tile_x(xT_half):
    t = xT_half.reshape(16, 128, T).transpose(1, 0, 2)
    return np.ascontiguousarray(t)


class _Sem:
    def __init__(self, nc, name):
        self.h = nc.alloc_semaphore(name)
        self.n = 0

    def inc(self, inst, k=1):
        inst.then_inc(self.h, k)
        self.n += k
        return self.n


def build_program(taps=(), use_cc=True):
    taps = set(taps)
    nc = bass.Bass("TRN2", target_bir_lowering=False)

    x_own = nc.dram_tensor("x_own", [128, 16, T], F32, kind="ExternalInput").ap()
    wqkv_t = nc.dram_tensor("wqkv_t", [48, 128, 2048], F32, kind="ExternalInput").ap()
    wo_t = nc.dram_tensor("wo_t", [16, 128, 2048], F32, kind="ExternalInput").ap()
    wg_t = nc.dram_tensor("wg_t", [IC, 128, 2048], F32, kind="ExternalInput").ap()
    wu_t = nc.dram_tensor("wu_t", [IC, 128, 2048], F32, kind="ExternalInput").ap()
    wd_t = nc.dram_tensor("wd_t", [16, 128, IC, 128], F32, kind="ExternalInput").ap()
    b_til = nc.dram_tensor("b_til", [128, 48], F32, kind="ExternalInput").ap()
    bo_til = nc.dram_tensor("bo_til", [128, 16], F32, kind="ExternalInput").ap()
    gs_til = nc.dram_tensor("gs_til", [128, IC], F32, kind="ExternalInput").ap()
    us_til = nc.dram_tensor("us_til", [128, IC], F32, kind="ExternalInput").ap()
    ds_til = nc.dram_tensor("ds_til", [128, 16], F32, kind="ExternalInput").ap()
    sflag = nc.dram_tensor("sflag", [128, 1], F32, kind="ExternalInput").ap()
    cflag = nc.dram_tensor("cflag", [128, 1], F32, kind="ExternalInput").ap()
    out_t = nc.dram_tensor("out_t", [128, 16, T], F32, kind="ExternalOutput").ap()
    cc_in = nc.dram_tensor("cc_in", [128, 2048], BF16, kind="Internal").ap()
    cc_out = nc.dram_tensor("cc_out", [128, 2048], BF16, kind="Internal").ap()

    dq_np, dk_np, dmt_np, lam_c = _consts()
    ident_d = nc.inline_tensor(np.eye(128, dtype=BF), "identc").ap()
    onescol_d = nc.inline_tensor(np.ones((128, 1), dtype=BF), "onescolc").ap()
    onesrow_d = nc.inline_tensor(np.ones((1, 128), dtype=np.float32), "onesrowc").ap()
    lamI_np = np.zeros((128, H * 128), dtype=np.float64)
    for h in range(H):
        lamI_np[:, h * 128:(h + 1) * 128][np.arange(128), np.arange(128)] = lam_c[h]
    lamI_d = nc.inline_tensor(np.ascontiguousarray(lamI_np.astype(BF)), "lamic").ap()
    lamc_np = np.repeat(np.array(lam_c, dtype=np.float64)[None, :], 128, 0)
    lamc_np = np.repeat(lamc_np, 128, 1).astype(BF)
    lamc_d = nc.inline_tensor(np.ascontiguousarray(lamc_np), "lamcc").ap()
    dq_d = nc.inline_tensor(dq_np, "dqc").ap()
    dk_d = nc.inline_tensor(dk_np, "dkc").ap()
    dmt_d = nc.inline_tensor(dmt_np, "dmtc").ap()

    tap_d = {}
    for name, shape in (
        ("kf_ctx", [128, 16, T]), ("vf_ctx", [128, 16, T]), ("S", [128, 2048]),
        ("qf", [128, 16, T]), ("kf", [128, 16, T]), ("vf", [128, 16, T]),
        ("attn", [128, 16, T]), ("r", [128, 16, T]), ("x2", [128, 16, T]),
        ("hdn1", [128, ICH, T]),
    ):
        if name in taps:
            tap_d[name] = nc.dram_tensor("tap_" + name, shape, F32,
                                         kind="ExternalOutput").ap()

    ident = nc.alloc_sbuf_tensor("ident_sb", [128, 128], BF16).ap()
    ones_col = nc.alloc_sbuf_tensor("onescol_sb", [128, 1], BF16).ap()
    ones_row = nc.alloc_sbuf_tensor("onesrow_sb", [1, 128], F32).ap()
    dq_all = nc.alloc_sbuf_tensor("dq_sb", [128, 2048], BF16).ap()
    dk_all = nc.alloc_sbuf_tensor("dk_sb", [128, 2048], BF16).ap()
    dmt_all = nc.alloc_sbuf_tensor("dmt_sb", [128, 2048], BF16).ap()
    lamI_sb = nc.alloc_sbuf_tensor("lamI_sb", [128, 2048], BF16).ap()
    b_sb = nc.alloc_sbuf_tensor("b_sb", [128, 48], F32).ap()
    bo_sb = nc.alloc_sbuf_tensor("bo_sb", [128, 16], F32).ap()
    gs_sb = nc.alloc_sbuf_tensor("gs_sb", [128, IC], F32).ap()
    us_sb = nc.alloc_sbuf_tensor("us_sb", [128, IC], F32).ap()
    ds_sb = nc.alloc_sbuf_tensor("ds_sb", [128, 16], F32).ap()
    flag_sb = nc.alloc_sbuf_tensor("flag_sb", [128, 1], F32).ap()
    cflag_sb = nc.alloc_sbuf_tensor("cflag_sb", [128, 1], F32).ap()
    s1_row = nc.alloc_sbuf_tensor("s1_row", [1, T], F32).ap()
    s2_row = nc.alloc_sbuf_tensor("s2_row", [1, T], F32).ap()
    s3_row = nc.alloc_sbuf_tensor("s3_row", [1, T], F32).ap()

    KB = 1024
    dyn = ((229344 - nc.sbuf_bytes_remaining) + 255) // 256 * 256
    assert dyn + 175 * KB <= 229344, dyn

    def at(name, shape, dtype, off):
        return nc.alloc_sbuf_tensor_at(name, shape, dtype, offset=dyn + off).ap()

    xb = at("xb", [128, 16, T], BF16, 0)
    attn_f = at("attn_f", [128, 16, T], BF16, 0)
    hdn_a = at("hdn_a", [128, 16, T], BF16, 0)
    kf = at("kf", [128, 16, T], BF16, 32 * KB)
    vf = at("vf", [128, 16, T], BF16, 64 * KB)
    r_f32 = at("r_f32", [128, 16, T], F32, 32 * KB)
    cstage = at("cstage", [128, 2048], BF16, 32 * KB)
    recv_sb = at("recv_sb", [128, 2048], BF16, 64 * KB)
    qf = at("qf", [128, 16, T], BF16, 96 * KB)
    x2b = at("x2b", [128, 16, T], BF16, 96 * KB)
    WO = 128 * KB
    wpA = at("wpA", [128, 3, 2048], BF16, WO + 0)
    outbuf = at("outbuf", [128, 8, T], F32, 96 * KB)
    wpB = at("wpB", [128, 2, 2048], BF16, WO + 8 * KB)
    xres = at("xres", [128, 2, T], F32, WO + 8 * KB)
    wpD = at("wpD", [128, 2, ICH * 128], BF16, WO + 16 * KB)
    ntmp = at("ntmp", [128, 2, T], F32, WO + 16 * KB)
    XO = WO + 27 * KB
    ks2 = at("ks2", [128, 2, 2048], BF16, WO + 0)
    dql = at("dql", [128, 2048], BF16, WO + 8 * KB)
    lamc_sb = at("lamc_sb", [128, 2048], BF16, XO + 0)
    vtm2 = at("vtm2", [128, 2, 2048], BF16, WO + 8 * KB)
    sm_sb = at("sm_sb", [128, 2048], BF16, XO + 8 * KB)
    qs_sb = at("qs_sb", [128, 2048], BF16, XO + 12 * KB)
    S_bf = at("S_bf", [128, 2048], BF16, XO + 16 * KB)
    hdn_b = at("hdn_b", [128, 6, T], BF16, XO + 0)
    rb_buf = at("rb_buf", [128, 2, T], BF16, XO + 12 * KB)
    sq_buf = at("sq_buf", [128, 2, T], BF16, XO + 16 * KB)
    silu_buf = at("silu_buf", [128, 2, T], BF16, XO + 12 * KB)
    mbc_sb = at("mbc_sb", [128, 2, T], F32, XO + 12 * KB)

    def hdn_tile(i):
        sl = i % ICH
        return hdn_a[:, sl, :] if sl < 16 else hdn_b[:, sl - 16, :]

    s_in = _Sem(nc, "s_in")
    s_w = _Sem(nc, "s_w")
    s_pe = _Sem(nc, "s_pe")
    s_ac = _Sem(nc, "s_ac")
    s_ve = _Sem(nc, "s_ve")
    s_st = _Sem(nc, "s_st")
    s_gp = _Sem(nc, "s_gp")

    stA = ExitStack()
    stT = ExitStack()
    psA = stA.enter_context(nc.psum_tensor("psA", [128, 2048], F32)).ap()
    psT = stT.enter_context(nc.psum_tensor("psT", [128, 2048], BF16)).ap()
    psT2 = stT.enter_context(nc.psum_tensor("psT2", [128, 2048], BF16)).ap()

    mk = {}

    def W(eng, *pairs):
        for s, v in pairs:
            if v:
                eng.wait_ge(s.h, v)

    with nc.Block() as blk:
        @blk.sync
        def _(sy):
            for dst, src in ((ident, ident_d), (ones_row, onesrow_d),
                             (dq_all, dq_d), (dk_all, dk_d), (dmt_all, dmt_d),
                             (b_sb, b_til), (bo_sb, bo_til), (gs_sb, gs_til),
                             (us_sb, us_til), (ds_sb, ds_til), (flag_sb, sflag),
                             (cflag_sb, cflag), (ones_col, onescol_d),
                             (lamI_sb, lamI_d), (lamc_sb, lamc_d)):
                s_in.inc(sy.dma_start(out=dst[:, :], in_=src[:, :]), 16)
        consts_done = s_in.n

        @blk.gpsimd
        def _(g):
            s_w.inc(g.dma_start(out=xb[:, :, :], in_=x_own[:, :, :]), 16)
        xb_w = s_w.n

        @blk.vector
        def _(v):
            W(v, (s_in, consts_done))
            mk["s_zero"] = s_ve.inc(v.memset(S_bf[:, :], 0.0))


        def proj_pass(n_pieces, piece_src, rhs3, dest_of, bias_of,
                      pe_first=(), gp_first=()):
            mm_at, ev_at = {}, {}
            for mi in range(n_pieces):
                @blk.gpsimd
                def _(g, mi=mi):
                    if mi == 0:
                        W(g, *gp_first)
                    if mi >= 3:
                        W(g, (s_pe, mm_at[mi - 3]))
                    s_w.inc(g.dma_start(out=wpA[:, mi % 3, :],
                                        in_=piece_src(mi)), 16)
                w_done = s_w.n

                @blk.tensor
                def _(te, mi=mi, w_done=w_done):
                    W(te, (s_w, w_done))
                    if mi == 0:
                        W(te, (s_in, consts_done), *pe_first)
                    if mi >= 2:
                        W(te, (s_ac, ev_at[mi - 2]))
                    half = mi % 2
                    for kc2 in range(KC):
                        w_ap = wpA[:, mi % 3, kc2 * 128:(kc2 + 1) * 128]
                        te.matmul(psA[:, half * 1024:half * 1024 + 512],
                                  w_ap, rhs3[:, kc2, 0:512],
                                  start=(kc2 == 0), stop=False)
                        mm = te.matmul(psA[:, half * 1024 + 512:(half + 1) * 1024],
                                       w_ap, rhs3[:, kc2, 512:1024],
                                       start=(kc2 == 0), stop=(kc2 == KC - 1))
                    mm_at[mi] = s_pe.inc(mm)

                @blk.scalar
                def _(s, mi=mi):
                    W(s, (s_pe, mm_at[mi]))
                    half = mi % 2
                    ins = s.activation(dest_of(mi),
                                       psA[:, half * 1024:(half + 1) * 1024],
                                       AF.Identity, bias=bias_of(mi), scale=1.0)
                    ev_at[mi] = s_ac.inc(ins)
            return mm_at, ev_at

        def scanA1(c):
            cs = slice(c * 128, (c + 1) * 128)

            @blk.tensor
            def _(te):
                W(te, mk.get("psT_free", (s_ve, 0)), mk.pop("kvf_ready", (s_ac, 0)))
                for h in range(H):
                    t = te.transpose(psT[:, h * 128:(h + 1) * 128], kf[:, h, cs],
                                     ident[:, :])
                mk["tk"] = s_pe.inc(t)
                W(te, mk.get("psT2_free", (s_ac, 0)))
                for h in range(H):
                    t = te.transpose(psT2[:, h * 128:(h + 1) * 128], vf[:, h, cs],
                                     ident[:, :])
                mk["tv"] = s_pe.inc(t)

        def scanA2(c):
            cs = slice(c * 128, (c + 1) * 128)
            p = c % 2
            slot_free = mk.get("ut1_c%d" % (c - 2), 0)

            @blk.tensor
            def _(te):
                for gg in range(2):
                    W(te, mk.get("psA%d_free" % gg, (s_ve, 0)))
                    for h in range(gg * 8, gg * 8 + 8):
                        t = te.matmul(psA[:, h * 128:(h + 1) * 128], kf[:, h, cs],
                                      qf[:, h, cs], start=(h % 4 == 0),
                                      stop=(h % 4 == 3), skip_group_check=True)
                    mk["sc%d" % gg] = s_pe.inc(t)

            @blk.vector
            def _(v, c=c, p=p, slot_free=slot_free):
                W(v, mk.pop("qf_ready", (s_ac, 0)))
                W(v, (s_pe, mk["tk"]), (s_pe, slot_free))
                mk["ks"] = s_ve.inc(
                    v.tensor_tensor(ks2[:, p, :], psT[:, :], dk_all[:, :], ALU.mult))
                v.tensor_tensor(qs_sb.rearrange("p (h i) -> p h i", h=H),
                                qf[:, :, cs],
                                dq_all.rearrange("p (h i) -> p h i", h=H),
                                ALU.mult)
                for gg in range(2):
                    gs_ = slice(gg * 1024, (gg + 1) * 1024)
                    W(v, (s_pe, mk["sc%d" % gg]))
                    mk["sm%d" % gg] = s_ve.inc(
                        v.tensor_tensor(sm_sb[:, gs_], psA[:, gs_],
                                        dmt_all[:, gs_], ALU.mult))
            mk["psT_free"] = (s_ve, mk["ks"])

            @blk.scalar
            def _(s, c=c, p=p, slot_free=slot_free):
                W(s, (s_pe, mk["tv"]), (s_pe, slot_free))
                mk["vtm"] = s_ac.inc(s.copy(vtm2[:, p, :], psT2[:, :]))
            mk["psT2_free"] = (s_ac, mk["vtm"])



        def scanB_at(c):
            cs = slice(c * 128, (c + 1) * 128)
            p = c % 2
            for gg in range(2):
                @blk.tensor
                def _(te, gg=gg):
                    W(te, (s_ve, mk["sm%d" % gg]))
                    if gg == 0:
                        W(te, (s_ac, mk["vtm"]))
                    for h in range(gg * 8, gg * 8 + 8):
                        hs = slice(h * 128, (h + 1) * 128)
                        te.matmul(psA[:, hs], S_bf[:, hs], qs_sb[:, hs],
                                  start=(h % 4 == 0), stop=False,
                                  skip_group_check=True)
                        t = te.matmul(psA[:, hs], vtm2[:, p, hs], sm_sb[:, hs],
                                      start=False, stop=(h % 4 == 3),
                                      skip_group_check=True)
                    mk["at%d" % gg] = s_pe.inc(t)

                @blk.scalar
                def _(s, gg=gg):
                    W(s, (s_pe, mk["at%d" % gg]))
                    mk["atev%d" % gg] = s_ac.inc(
                        s.copy(attn_f[:, gg * 8:(gg + 1) * 8, cs],
                               psA[:, gg * 1024:(gg + 1) * 1024]
                               .rearrange("p (h i) -> p h i", h=8)))

        def scanB_ut(c):
            p = c % 2
            for gg in range(2):
                @blk.tensor
                def _(te, gg=gg):
                    W(te, (s_ac, mk["atev%d" % gg]))
                    for h in range(gg * 8, gg * 8 + 8):
                        hs = slice(h * 128, (h + 1) * 128)
                        te.matmul(psA[:, hs], ks2[:, p, hs], vtm2[:, p, hs],
                                  start=(h % 4 == 0), stop=False,
                                  skip_group_check=True)
                        t = te.matmul(psA[:, hs], lamI_sb[:, hs], S_bf[:, hs],
                                      start=False, stop=(h % 4 == 3),
                                      skip_group_check=True)
                    mk["ut%d" % gg] = s_pe.inc(t)
            mk["ut1_c%d" % c] = mk["ut1"]

            for gg in range(2):
                @blk.scalar
                def _(s, gg=gg, c=c):
                    W(s, (s_pe, mk["ut%d" % gg]))
                    gsl = slice(gg * 1024, (gg + 1) * 1024)
                    mk["supd%d" % gg] = s_ac.inc(s.copy(S_bf[:, gsl], psA[:, gsl]))
                mk["psA%d_free" % gg] = (s_ac, mk["supd%d" % gg])

        def own_dest(mi):
            return (qf, kf, vf)[mi // 16][:, mi % 16, :]
        mm3, ev3 = proj_pass(
            48, lambda mi: wqkv_t[mi, :, :], xb,
            own_dest, lambda mi: b_sb[:, mi:mi + 1])
        seg3_end_pe = s_pe.n

        for tname, tsrc in (("qf", qf), ("kf", kf), ("vf", vf)):
            if tname in taps:
                @blk.gpsimd
                def _(g, tname=tname, tsrc=tsrc):
                    W(g, (s_ac, ev3[47]))
                    s_st.inc(g.dma_start(out=tap_d[tname][:, :, :],
                                         in_=tsrc[:, :, :]), 16)

        mk["psT_free"] = (s_ve, 0)
        mk["psA0_free"] = (s_ac, ev3[47])
        mk["psA1_free"] = (s_ac, ev3[47])
        mk["kvf_ready"] = (s_ac, ev3[47])
        mk["qf_ready"] = (s_ac, ev3[47])
        scanA1(0)
        scanA2(0)
        for c in range(NCH):
            scanB_at(c)
            if c + 1 < NCH:
                scanA1(c + 1)
            scanB_ut(c)
            if c + 1 < NCH:
                scanA2(c + 1)
        scan_end_pe = mk["ut1"]
        scan_supd = mk["supd1"]
        attn_all = mk["supd1"]

        stT.close()
        stC = ExitStack()
        psC = stC.enter_context(nc.psum_tensor("psC", [128, 2048], F32)).ap()

        s_cc = _Sem(nc, "s_cc")

        @blk.vector
        def _(v):
            W(v, (s_ac, mk["supd1"]))
            mk["cst"] = s_ve.inc(
                v.tensor_scalar_mul(cstage[:, :], S_bf[:, :], cflag_sb[:, 0:1]))

        @blk.sync
        def _(sy):
            W(sy, (s_ve, mk["cst"]))
            mk["cc_st"] = s_st.inc(sy.dma_start(out=cc_in[:, :], in_=cstage[:, :]), 16)

        @blk.gpsimd
        def _(g):
            W(g, (s_st, mk["cc_st"]))
            if use_cc:
                s_cc.inc(g.collective_compute(
                    "AllReduce", ALU.add,
                    replica_groups=[[0, 1], [2, 3], [4, 5], [6, 7]],
                    ins=[cc_in.opt()], outs=[cc_out.opt()]))
            else:
                s_cc.inc(g.dma_start(out=cc_out[:, :], in_=cc_in[:, :]), 16)

        @blk.sync
        def _(sy):
            W(sy, (s_cc, s_cc.n))
            mk["rc_ld"] = s_in.inc(sy.dma_start(out=recv_sb[:, :], in_=cc_out[:, :]), 16)

        @blk.vector
        def _(v):
            W(v, (s_in, mk["rc_ld"]))
            mk["sprev"] = s_ve.inc(
                v.tensor_scalar_mul(S_bf[:, :], recv_sb[:, :], flag_sb[:, 0:1]))

        wo_pre = {}

        @blk.gpsimd
        def _(g):
            W(g, (s_pe, scan_end_pe))
            for j in (0, 1):
                wo_pre[j] = s_w.inc(
                    g.dma_start(out=wpA[:, j % 2, :], in_=wo_t[j, :, :]), 16)

        ofh_at = {}

        def outproj_firsthalf(j):
            @blk.tensor
            def _(te, j=j):
                W(te, (s_w, wo_pre[j]))
                if j == 0:
                    W(te, (s_ac, scan_supd), (s_ve, mk["corr_h1"]))
                half = j % 2
                for kc2 in range(KC):
                    w_ap = wpA[:, half, kc2 * 128:(kc2 + 1) * 128]
                    t = te.matmul(psA[:, half * 1024:half * 1024 + 512], w_ap,
                                  attn_f[:, kc2, 0:512], start=(kc2 == 0),
                                  stop=(kc2 == KC - 1), skip_group_check=True)
                ofh_at[j] = s_pe.inc(t)

        for l in range(NCH):
            ls = slice(l * 128, (l + 1) * 128)

            @blk.vector
            def _(v, l=l):
                if l == 0:
                    v.tensor_copy(dql[:, :], dq_all[:, :])
                else:
                    v.tensor_tensor(dql[:, :], dql[:, :], lamc_sb[:, :], ALU.mult)
                mk["cqs"] = s_ve.inc(
                    v.tensor_tensor(qs_sb.rearrange("p (h i) -> p h i", h=H),
                                    qf[:, :, ls],
                                    dql.rearrange("p (h i) -> p h i", h=H),
                                    ALU.mult))

            for gg in range(2):
                @blk.tensor
                def _(te, l=l, gg=gg):
                    if gg == 0:
                        W(te, (s_ve, mk["cqs"]))
                        if l == 0:
                            W(te, (s_ve, mk["sprev"]))
                    for h in range(gg * 8, gg * 8 + 8):
                        hs = slice(h * 128, (h + 1) * 128)
                        t = te.matmul(psC[:, hs], S_bf[:, hs], qs_sb[:, hs],
                                      start=(h % 4 == 0), stop=(h % 4 == 3),
                                      skip_group_check=True)
                    mk["cmm%d" % gg] = s_pe.inc(t)

                @blk.vector
                def _(v, l=l, gg=gg):
                    W(v, (s_pe, mk["cmm%d" % gg]))
                    gsl = slice(gg * 1024, (gg + 1) * 1024)
                    t = v.scalar_tensor_tensor(
                        attn_f[:, gg * 8:(gg + 1) * 8, ls],
                        psC[:, gsl].rearrange("p (h i) -> p h i", h=8), 1.0,
                        attn_f[:, gg * 8:(gg + 1) * 8, ls],
                        ALU.mult, ALU.add)
                    mk["corr%d" % gg] = s_ve.inc(t)
            if l == 3:
                mk["corr_h1"] = mk["corr1"]
            if l in (4, 5):
                outproj_firsthalf(l - 4)
        attn_all = mk["corr1"]

        if "attn" in taps:
            @blk.gpsimd
            def _(g):
                W(g, (s_ve, attn_all))
                s_st.inc(g.dma_start(out=tap_d["attn"][:, :, :], in_=attn_f[:, :, :]), 16)

        stC.close()
        stS = ExitStack()
        psS1 = stS.enter_context(nc.psum_tensor("psS1", [1, T], F32)).ap()
        psS2 = stS.enter_context(nc.psum_tensor("psS2", [1, T], F32)).ap()

        rb_at, sq_at, st1_at = {}, {}, {}
        r_at, omm_at = {}, {}

        def stats_mm(j, ps1, ps2):
            @blk.tensor
            def _(te, j=j):
                W(te, (s_ve, rb_at[j]))
                te.matmul(ps1[0:1, 0:512], ones_col[:, :], rb_buf[:, j % 2, 0:512],
                          start=(j == 0), stop=(j == 15), skip_group_check=True)
                te.matmul(ps1[0:1, 512:1024], ones_col[:, :], rb_buf[:, j % 2, 512:1024],
                          start=(j == 0), stop=(j == 15), skip_group_check=True)
                W(te, (s_ac, sq_at[j]))
                te.matmul(ps2[0:1, 0:512], ones_col[:, :], sq_buf[:, j % 2, 0:512],
                          start=(j == 0), stop=(j == 15), skip_group_check=True)
                t = te.matmul(ps2[0:1, 512:1024], ones_col[:, :],
                              sq_buf[:, j % 2, 512:1024],
                              start=(j == 0), stop=(j == 15), skip_group_check=True)
                st1_at[j] = s_pe.inc(t)

        def ln_rows_and_bcast(ps1, ps2, psBC):
            inv = 1.0 / DM
            @blk.vector
            def _(v):
                W(v, (s_pe, st1_at[15]))
                v.tensor_scalar_mul(s1_row[0:1, :], ps1[0:1, :], inv)
                v.tensor_scalar_mul(s2_row[0:1, :], ps2[0:1, :], inv)
                v.tensor_tensor(s3_row[0:1, :], s1_row[0:1, :], s1_row[0:1, :], ALU.mult)
                v.tensor_tensor(s2_row[0:1, :], s2_row[0:1, :], s3_row[0:1, :],
                                ALU.subtract)
                t = v.tensor_scalar_add(s2_row[0:1, :], s2_row[0:1, :], EPS)
                mk["var"] = s_ve.inc(t)

            @blk.scalar
            def _(s):
                W(s, (s_ve, mk["var"]))
                s.activation(s3_row[0:1, :], s2_row[0:1, :], AF.Ln)
                mk["rstd"] = s_ac.inc(
                    s.activation(s3_row[0:1, :], s3_row[0:1, :], AF.Exp, scale=-0.5))

            @blk.tensor
            def _(te):
                W(te, (s_ac, mk["rstd"]))
                for hh in range(2):
                    te.matmul(psBC[:, hh * 512:hh * 512 + 512], ones_row[0:1, :],
                              s1_row[0:1, hh * 512:hh * 512 + 512],
                              start=True, stop=True, skip_group_check=True)
                    t = te.matmul(psBC[:, 1024 + hh * 512:1536 + hh * 512],
                                  ones_row[0:1, :],
                                  s3_row[0:1, hh * 512:hh * 512 + 512],
                                  start=True, stop=True, skip_group_check=True)
                mk["bc"] = s_pe.inc(t)

            @blk.vector
            def _(v):
                W(v, (s_pe, mk["bc"]))
                v.tensor_copy(mbc_sb[:, 0, :], psBC[:, 0:1024])
                t = v.tensor_copy(mbc_sb[:, 1, :], psBC[:, 1024:2048])
                mk["mbc"] = s_ve.inc(t)

        for j in range(16):
            if j >= 2:
                @blk.gpsimd
                def _(g, j=j):
                    W(g, (s_pe, omm_at[j - 2]))
                    s_w.inc(g.dma_start(out=wpA[:, j % 2, :], in_=wo_t[j, :, :]), 16)
            wj = s_w.n

            @blk.sync
            def _(sy, j=j):
                if j <= 1:
                    W(sy, (s_ve, attn_all))
                if j >= 2:
                    W(sy, (s_ve, r_at[j - 2]))
                s_in.inc(sy.dma_start(out=xres[:, j % 2, :], in_=x_own[:, j, :]), 16)
            xj = s_in.n

            @blk.tensor
            def _(te, j=j, wj=wj):
                W(te, (s_w, wj))
                if j == 0:
                    W(te, (s_ve, attn_all))
                if j >= 2:
                    W(te, (s_ve, r_at[j - 2]))
                half = j % 2
                if j >= 2:
                    for kc2 in range(KC):
                        w_ap = wpA[:, half, kc2 * 128:(kc2 + 1) * 128]
                        te.matmul(psA[:, half * 1024:half * 1024 + 512], w_ap,
                                  attn_f[:, kc2, 0:512], start=(kc2 == 0), stop=False)
                for kc2 in range(KC):
                    w_ap = wpA[:, half, kc2 * 128:(kc2 + 1) * 128]
                    t = te.matmul(psA[:, half * 1024 + 512:(half + 1) * 1024], w_ap,
                                  attn_f[:, kc2, 512:1024], start=(kc2 == 0),
                                  stop=(kc2 == KC - 1))
                omm_at[j] = s_pe.inc(t)

            @blk.vector
            def _(v, j=j, xj=xj):
                W(v, (s_pe, omm_at[j]), (s_in, xj))
                half = j % 2
                v.scalar_tensor_tensor(r_f32[:, j, :],
                                       psA[:, half * 1024:(half + 1) * 1024],
                                       bo_sb[:, j:j + 1], xres[:, half, :],
                                       ALU.add, ALU.add)
                t = v.tensor_copy(rb_buf[:, j % 2, :], r_f32[:, j, :])
                r_at[j] = rb_at[j] = s_ve.inc(t)

            @blk.scalar
            def _(s, j=j):
                W(s, (s_ve, rb_at[j]))
                sq_at[j] = s_ac.inc(
                    s.activation(sq_buf[:, j % 2, :], rb_buf[:, j % 2, :], AF.Square))
            if j >= 1:
                stats_mm(j - 1, psS1, psS2)
            if j == 15:
                stats_mm(15, psS1, psS2)

        if "r" in taps:
            @blk.sync
            def _(sy):
                W(sy, (s_ve, r_at[15]))
                s_st.inc(sy.dma_start(out=tap_d["r"][:, :, :], in_=r_f32[:, :, :]), 16)

        ln_rows_and_bcast(psS1, psS2, psA)
        bc1 = mk["bc"]

        pre_w = {}

        @blk.gpsimd
        def _(g):
            W(g, (s_pe, omm_at[15]), (s_ve, r_at[15]))
            for i in (0, 1):
                s_w.inc(g.dma_start(out=wpA[:, i % 2, :], in_=wg_t[i, :, :]), 16)
                s_w.inc(g.dma_start(out=wpB[:, i % 2, :], in_=wu_t[i, :, :]), 16)
                pre_w[i] = s_w.n

        x2b_rdy = {}
        for j in range(13):
            @blk.vector
            def _(v, j=j):
                if j == 0:
                    W(v, (s_ve, mk["mbc"]))
                v.tensor_tensor(ntmp[:, 0, :], r_f32[:, j, :],
                                mbc_sb[:, 0, :], ALU.subtract)
                x2b_rdy[j] = ("v", s_ve.inc(
                    v.tensor_tensor(x2b[:, j, :], ntmp[:, 0, :],
                                    mbc_sb[:, 1, :], ALU.mult)))
        for j in range(13, 16):
            @blk.gpsimd
            def _(g, j=j):
                if j == 13:
                    g.wait_ge(s_ve.h, mk["mbc"])
                g.tensor_tensor(ntmp[:, 1, :], r_f32[:, j, :],
                                mbc_sb[:, 0, :], ALU.subtract)
                x2b_rdy[j] = ("g", s_gp.inc(
                    g.tensor_tensor(x2b[:, j, :], ntmp[:, 1, :],
                                    mbc_sb[:, 1, :], ALU.mult)))
        ln1_doneV = x2b_rdy[12][1]
        ln1_doneP = x2b_rdy[15][1]
        ln1_done = ln1_doneV

        if "x2" in taps:
            @blk.gpsimd
            def _(g):
                g.wait_ge(s_ve.h, ln1_doneV)
                g.wait_ge(s_gp.h, ln1_doneP)
                s_st.inc(g.dma_start(out=tap_d["x2"][:, :, :], in_=x2b[:, :, :]), 16)

        stS.close()
        stA.close()

        gu_mm_at, hdn_at = {}, {}
        dmm_at, r2_at = {}, {}

        def gate_up(psG, i_lo, i_hi, gp_first=(), pe_first=(), preload=None):
            for i in range(i_lo, i_hi):
                io = i - i_lo

                if preload is not None and io < 2:
                    wi = preload[io]
                else:
                    @blk.gpsimd
                    def _(g, i=i, io=io):
                        if io == 0:
                            W(g, *gp_first)
                        if io >= 2:
                            W(g, (s_pe, gu_mm_at[i - 2]))
                        s_w.inc(g.dma_start(out=wpA[:, i % 2, :], in_=wg_t[i, :, :]), 16)
                        s_w.inc(g.dma_start(out=wpB[:, i % 2, :], in_=wu_t[i, :, :]), 16)
                    wi = s_w.n

                @blk.tensor
                def _(te, i=i, io=io, wi=wi):
                    W(te, (s_w, wi))
                    if io == 0:
                        W(te, *pe_first)
                    if io >= 2:
                        W(te, (s_ve, hdn_at[i - 2]))
                    q = (i % 2) * 2048
                    for kc2 in range(KC):
                        if io == 0 and i == 0 and kc2 in x2b_rdy:
                            eng, vv = x2b_rdy[kc2]
                            W(te, (s_ve if eng == "v" else s_gp, vv))
                        wg_ap = wpA[:, i % 2, kc2 * 128:(kc2 + 1) * 128]
                        wu_ap = wpB[:, i % 2, kc2 * 128:(kc2 + 1) * 128]
                        te.matmul(psG[:, q:q + 512], wg_ap, x2b[:, kc2, 0:512],
                                  start=(kc2 == 0), stop=False)
                        te.matmul(psG[:, q + 512:q + 1024], wg_ap, x2b[:, kc2, 512:1024],
                                  start=(kc2 == 0), stop=(kc2 == KC - 1))
                        te.matmul(psG[:, q + 1024:q + 1536], wu_ap, x2b[:, kc2, 0:512],
                                  start=(kc2 == 0), stop=False)
                        t = te.matmul(psG[:, q + 1536:q + 2048], wu_ap,
                                      x2b[:, kc2, 512:1024],
                                      start=(kc2 == 0), stop=(kc2 == KC - 1))
                    gu_mm_at[i] = s_pe.inc(t)

                @blk.scalar
                def _(s, i=i):
                    W(s, (s_pe, gu_mm_at[i]))
                    q = (i % 2) * 2048
                    mk["sil"] = s_ac.inc(
                        s.activation(silu_buf[:, i % 2, :], psG[:, q:q + 1024],
                                     AF.Silu, scale=gs_sb[:, i:i + 1]))

                @blk.vector
                def _(v, i=i):
                    W(v, (s_ac, mk["sil"]))
                    q = (i % 2) * 2048
                    hdn_at[i] = s_ve.inc(
                        v.scalar_tensor_tensor(hdn_tile(i), psG[:, q + 1024:q + 2048],
                                               us_sb[:, i:i + 1], silu_buf[:, i % 2, :],
                                               ALU.mult, ALU.mult))

        def down_pass(psD, ih, ps1=None, ps2=None, gp_first=()):
            for j in range(16):
                @blk.gpsimd
                def _(g, j=j):
                    if j == 0:
                        W(g, *gp_first)
                    if j >= 2:
                        W(g, (s_pe, dmm_at[(ih, j - 2)]))
                    s_w.inc(g.dma_start(
                        out=wpD[:, j % 2, :],
                        in_=wd_t[j, :, ih * ICH:(ih + 1) * ICH, :]), 16)
                wj = s_w.n

                @blk.tensor
                def _(te, j=j, wj=wj):
                    W(te, (s_w, wj))
                    if j == 0:
                        W(te, (s_ve, hdn_at[ih * ICH + ICH - 1]))
                    if j >= 2:
                        W(te, (s_ve, r2_at[(ih, j - 2)]))
                    half = j % 2
                    for ii in range(ICH):
                        w_ap = wpD[:, half, ii * 128:(ii + 1) * 128]
                        ht = hdn_tile(ih * ICH + ii)
                        te.matmul(psD[:, half * 1024:half * 1024 + 512], w_ap,
                                  ht[:, 0:512], start=(ii == 0), stop=False)
                        t = te.matmul(psD[:, half * 1024 + 512:(half + 1) * 1024], w_ap,
                                      ht[:, 512:1024], start=(ii == 0),
                                      stop=(ii == ICH - 1))
                    dmm_at[(ih, j)] = s_pe.inc(t)

                @blk.vector
                def _(v, j=j):
                    W(v, (s_pe, dmm_at[(ih, j)]))
                    half = j % 2
                    if ih == 0:
                        t = v.scalar_tensor_tensor(
                            r_f32[:, j, :], psD[:, half * 1024:(half + 1) * 1024],
                            ds_sb[:, j:j + 1], x2b[:, j, :], ALU.mult, ALU.add)
                    else:
                        v.scalar_tensor_tensor(
                            r_f32[:, j, :], psD[:, half * 1024:(half + 1) * 1024],
                            ds_sb[:, j:j + 1], r_f32[:, j, :], ALU.mult, ALU.add)
                        t = v.tensor_copy(rb_buf[:, j % 2, :], r_f32[:, j, :])
                    r2_at[(ih, j)] = s_ve.inc(t)

                if ih == 1:
                    rb_at[j] = r2_at[(1, j)]

                    @blk.scalar
                    def _(s, j=j):
                        W(s, (s_ve, rb_at[j]))
                        sq_at[j] = s_ac.inc(
                            s.activation(sq_buf[:, j % 2, :], rb_buf[:, j % 2, :],
                                         AF.Square))
                    if j >= 1:
                        stats_mm(j - 1, ps1, ps2)
                    if j == 15:
                        stats_mm(15, ps1, ps2)

        st6 = ExitStack()
        psG = st6.enter_context(nc.psum_tensor("psG", [128, 4096], F32)).ap()
        gate_up(psG, 0, ICH, preload=pre_w)
        if "hdn1" in taps:
            @blk.gpsimd
            def _(g):
                W(g, (s_ve, hdn_at[ICH - 1]))
                for i in range(ICH):
                    s_st.inc(g.dma_start(out=tap_d["hdn1"][:, i, :], in_=hdn_tile(i)), 16)

        st6.close()
        st7 = ExitStack()
        psD = st7.enter_context(nc.psum_tensor("psD", [128, 2048], F32)).ap()
        down_pass(psD, 0, gp_first=((s_ve, ln1_doneV), (s_gp, ln1_doneP)))
        seg7_r2 = r2_at[(0, 15)]

        st7.close()
        st8 = ExitStack()
        psG2 = st8.enter_context(nc.psum_tensor("psG2", [128, 4096], F32)).ap()
        gate_up(psG2, ICH, IC,
                gp_first=((s_pe, gu_mm_at[ICH - 1]),),
                pe_first=((s_ve, seg7_r2),))

        st8.close()
        st9 = ExitStack()
        psD2 = st9.enter_context(nc.psum_tensor("psD2", [128, 2048], F32)).ap()
        psS1b = st9.enter_context(nc.psum_tensor("psS1b", [1, T], F32)).ap()
        psS2b = st9.enter_context(nc.psum_tensor("psS2b", [1, T], F32)).ap()
        down_pass(psD2, 1, ps1=psS1b, ps2=psS2b,
                  gp_first=((s_pe, dmm_at[(0, 15)]),))

        ln_rows_and_bcast(psS1b, psS2b, psD2)
        bc2 = mk["bc"]

        out_st = {}
        onrm = {}
        for j in range(13):
            @blk.vector
            def _(v, j=j):
                if j == 0:
                    W(v, (s_ve, mk["mbc"]))
                if j >= 4:
                    W(v, (s_st, out_st[j - 4]))
                v.tensor_tensor(ntmp[:, 0, :], r_f32[:, j, :],
                                mbc_sb[:, 0, :], ALU.subtract)
                onrm[j] = ("v", s_ve.inc(
                    v.tensor_tensor(outbuf[:, j % 4, :], ntmp[:, 0, :],
                                    mbc_sb[:, 1, :], ALU.mult)))

            @blk.sync
            def _(sy, j=j):
                W(sy, (s_ve, onrm[j][1]))
                out_st[j] = s_st.inc(
                    sy.dma_start(out=out_t[:, j, :], in_=outbuf[:, j % 4, :]), 16)

        for j in range(13, 16):
            @blk.gpsimd
            def _(g, j=j):
                if j == 13:
                    g.wait_ge(s_ve.h, mk["mbc"])
                if j >= 15:
                    g.wait_ge(s_st.h, out_st[j - 4])
                g.tensor_tensor(ntmp[:, 1, :], r_f32[:, j, :],
                                mbc_sb[:, 0, :], ALU.subtract)
                onrm[j] = ("g", s_gp.inc(
                    g.tensor_tensor(outbuf[:, 4 + j % 4, :], ntmp[:, 1, :],
                                    mbc_sb[:, 1, :], ALU.mult)))

            @blk.sync
            def _(sy, j=j):
                W(sy, (s_gp, onrm[j][1]))
                out_st[j] = s_st.inc(
                    sy.dma_start(out=out_t[:, j, :], in_=outbuf[:, 4 + j % 4, :]), 16)

        @blk.sync
        def _(sy):
            W(sy, (s_st, s_st.n))
        st9.close()

    return nc


def _prepare(inputs):
    wqkv_t = _tile_w(np.asarray(inputs["L0_attn_w_qkv"], np.float32))
    wo_t = _tile_w(np.asarray(inputs["L0_attn_w_out"], np.float32))
    wg_t = _tile_w(np.asarray(inputs["L0_ffn_gate_w"], np.float32))
    wu_t = _tile_w(np.asarray(inputs["L0_ffn_up_w"], np.float32))
    wd_t = _tile_wd(np.asarray(inputs["L0_ffn_down_w"], np.float32))
    b_til = np.ascontiguousarray(
        np.asarray(inputs["L0_attn_b_qkv"], np.float32).reshape(48, 128).T)
    bo_til = np.ascontiguousarray(
        np.asarray(inputs["L0_attn_b_out"], np.float32).reshape(16, 128).T)
    gs_til = np.ascontiguousarray(
        np.asarray(inputs["L0_ffn_gate_s"], np.float32).reshape(IC, 128).T)
    us_til = np.ascontiguousarray(
        np.asarray(inputs["L0_ffn_up_s"], np.float32).reshape(IC, 128).T)
    ds_til = np.ascontiguousarray(
        np.asarray(inputs["L0_ffn_down_s"], np.float32).reshape(16, 128).T)
    return dict(wqkv_t=wqkv_t, wo_t=wo_t, wg_t=wg_t, wu_t=wu_t, wd_t=wd_t,
                b_til=b_til, bo_til=bo_til, gs_til=gs_til, us_til=us_til,
                ds_til=ds_til)


def make_in_maps(inputs):
    shared = _prepare(inputs)
    x = np.asarray(inputs["L0_attn_x"], np.float32)
    flag0 = np.zeros((128, 1), np.float32)
    flag1 = np.ones((128, 1), np.float32)
    in_maps = []
    for c in range(8):
        b, half = c // 2, c % 2
        xT = np.ascontiguousarray(x[b].T)
        m = dict(shared)
        m["x_own"] = _tile_x(xT[:, half * T:(half + 1) * T])
        m["sflag"] = flag1 if half == 1 else flag0
        m["cflag"] = flag0 if half == 1 else flag1
        in_maps.append(m)
    return in_maps


def kernel(**inputs):
    nc = build_program()
    in_maps = make_in_maps(inputs)
    res = run_bass_kernel_spmd(nc, in_maps, core_ids=list(range(8)))
    out = np.empty((BATCH, SEQ, DM), np.float32)
    for c in range(8):
        b, half = c // 2, c % 2
        ot = res.results[c]["out_t"]
        out[b, half * T:(half + 1) * T, :] = ot.transpose(1, 0, 2).reshape(DM, T).T
    return out
